# version 11
# baseline (speedup 1.0000x reference)
import numpy as np
from contextlib import ExitStack

import concourse.bass as bass
import concourse.mybir as mybir
from concourse.bass_utils import run_bass_kernel_spmd

F32 = mybir.dt.float32
BF16 = mybir.dt.bfloat16
I32 = mybir.dt.int32
AF = mybir.ActivationFunctionType
ALU = mybir.AluOpType
AX = mybir.AxisListType.X

C = 1024
S = 4096
NCH = 8
QC = 512
NQC = S // QC
D = 64
NPAIR = 8
KCN = 32
EPS = 1e-5
INV32 = 1.0 / 32.0

MAGIC_RSQRT = 0x5F3759DF
MAGIC_RECIP = 0x7EF477D5


class Sem:
    def __init__(self, h):
        self.h = h
        self.n = 0


class B:
    ENGS = ("sync", "tensor", "scalar", "vector", "gpsimd")

    def __init__(self):
        self.q = {e: [] for e in self.ENGS}
        self.free = {}

    def on(self, eng, fn):
        self.q[eng].append(fn)

    def wait(self, eng, sem, val):
        if val > 0:
            self.on(eng, lambda e, s=sem, v=val: e.wait_ge(s.h, v))

    def wf(self, eng, region):
        if region in self.free:
            sem, tick = self.free[region]
            self.wait(eng, sem, tick)

    def sf(self, region, sem, tick):
        self.free[region] = (sem, tick)

    def op(self, eng, sem, fn, inc=1):
        sem.n += inc
        self.on(eng, lambda e, f=fn, s=sem, i=inc: f(e).then_inc(s.h, i))
        return sem.n

    def raw(self, eng, fn):
        self.on(eng, fn)

    def dma(self, eng, sem, fn):
        sem.n += 16
        self.on(eng, lambda e, f=fn, s=sem: f(e).then_inc(s.h, 16))
        return sem.n

    def run(self, block):
        for eng in self.ENGS:
            fns = self.q[eng]
            def mk(fns):
                def body(e):
                    for f in fns:
                        f(e)
                return body
            getattr(block, eng)(mk(fns))


def build_nc(debug=False, nqc=NQC, do_b=True, stage='full'):
    nc = bass.Bass("TRN2", target_bir_lowering=False)

    xt = nc.dram_tensor("xt", [C, S], F32, kind="ExternalInput").ap()
    wrt = nc.dram_tensor("wrt", [C, C], F32, kind="ExternalInput").ap()
    wkt = nc.dram_tensor("wkt", [C, C], F32, kind="ExternalInput").ap()
    wvt = nc.dram_tensor("wvt", [C, C], F32, kind="ExternalInput").ap()
    woutt = nc.dram_tensor("woutt", [C, C], F32, kind="ExternalInput").ap()
    wqa = nc.dram_tensor("wqa", [128, D], F32, kind="ExternalInput").ap()
    wka = nc.dram_tensor("wka", [128, D], F32, kind="ExternalInput").ap()
    wva = nc.dram_tensor("wva", [128, D], F32, kind="ExternalInput").ap()
    bout = nc.dram_tensor("boutp", [128, NCH], F32, kind="ExternalInput").ap()
    murh = nc.dram_tensor("murh", [128, NCH], F32, kind="ExternalInput").ap()
    mukh = nc.dram_tensor("mukh", [128, NCH], F32, kind="ExternalInput").ap()
    yt = nc.dram_tensor("yt", [C, S], F32, kind="ExternalOutput").ap()
    scratch_kind = "ExternalOutput" if debug else "Internal"
    x1s = nc.dram_tensor("x1s", [C, S], F32, kind=scratch_kind).ap()
    dbgn = nc.dram_tensor("dbgn", [128, NPAIR], F32, kind=scratch_kind).ap()
    dbgr = nc.dram_tensor("dbgr", [8, QC], F32, kind=scratch_kind).ap()
    dbga = nc.dram_tensor("dbga", [128, S], BF16, kind=scratch_kind).ap()
    dbga2 = nc.dram_tensor("dbga2", [128, 1024], F32, kind=scratch_kind).ap()
    dbgp = nc.dram_tensor("dbgp", [128, KCN * 1024], BF16, kind=scratch_kind).ap()
    dbgs = nc.dram_tensor("dbgs", [2, 16], F32, kind=scratch_kind).ap()
    ks = nc.dram_tensor("ks", [NPAIR, 128, S], BF16, kind=scratch_kind).ap()
    vs = nc.dram_tensor("vs", [NPAIR, 128, KCN * 130], BF16, kind=scratch_kind).ap()
    xn1s = nc.dram_tensor("xn1s", [C, S], BF16, kind=scratch_kind).ap()

    with ExitStack() as ctx:
        en = ctx.enter_context
        wr_sb = en(nc.sbuf_tensor("wr_sb", [128, NCH * C], BF16))
        wk_sb = en(nc.sbuf_tensor("wk_sb", [128, NCH * C], BF16))
        wv_sb = en(nc.sbuf_tensor("wv_sb", [128, NCH * C], BF16))
        wout_sb = en(nc.sbuf_tensor("wout_sb", [128, NCH * C], BF16))
        wqa_sb = en(nc.sbuf_tensor("wqa_sb", [128, D], BF16))
        wka_sb = en(nc.sbuf_tensor("wka_sb", [128, D], BF16))
        wva_sb = en(nc.sbuf_tensor("wva_sb", [128, D], BF16))
        bout_sb = en(nc.sbuf_tensor("bout_sb", [128, NCH], F32))
        mur_sb = en(nc.sbuf_tensor("mur_sb", [128, NCH], F32))
        muk_sb = en(nc.sbuf_tensor("muk_sb", [128, NCH], F32))
        ones_sb = en(nc.sbuf_tensor("ones_sb", [128, 128], BF16))
        negc_sb = en(nc.sbuf_tensor("negc_sb", [128, NPAIR], F32))
        bnd1_sb = en(nc.sbuf_tensor("bnd1_sb", [128, NCH], BF16))
        bnd2_sb = en(nc.sbuf_tensor("bnd2_sb", [128, NCH], BF16))
        qmax_sb = en(nc.sbuf_tensor("qmax_sb", [1, 16], F32))
        kmax_sb = en(nc.sbuf_tensor("kmax_sb", [1, 16], F32))
        fr = en(nc.sbuf_tensor("fr", [1, 4 * QC], F32))
        br = en(nc.sbuf_tensor("br", [1, 4 * QC], BF16))
        u64 = en(nc.sbuf_tensor("u64", [128, 2 * QC], F32))
        ub64 = en(nc.sbuf_tensor("ub64", [128, 4 * QC], BF16))
        avout = [en(nc.sbuf_tensor(f"avout{i}", [128, 2 * QC], F32)) for i in range(2)]
        xb = [en(nc.sbuf_tensor(f"xb{i}", [128, S], F32)) for i in range(2)]
        xn = en(nc.sbuf_tensor("xn", [128, NCH * (QC + 1)], BF16))
        xn1 = en(nc.sbuf_tensor("xn1", [128, S], BF16))
        b1 = en(nc.sbuf_tensor("b1", [128, S], BF16))
        b2 = en(nc.sbuf_tensor("b2", [128, S], BF16))
        t0 = en(nc.sbuf_tensor("t0", [128, QC], F32))
        tb_ = [en(nc.sbuf_tensor(f"tb{i}", [128, QC], F32)) for i in range(2)]
        cast = [en(nc.sbuf_tensor(f"cast{i}", [128, 2 * QC], BF16)) for i in range(2)]
        stgk = [en(nc.sbuf_tensor("stgk0", [128, QC], BF16))] * 2
        stgv = [en(nc.sbuf_tensor("stgv0", [128, 520], BF16))] * 2
        kp = [en(nc.sbuf_tensor(f"kp{i}", [128, S], BF16)) for i in range(2)]
        vp = [en(nc.sbuf_tensor(f"vp{i}", [128, KCN * 130], BF16)) for i in range(2)]
        pt = [en(nc.sbuf_tensor(f"pt{i}", [128, 1024], BF16)) for i in range(2)]
        q_sb = en(nc.sbuf_tensor("q_sb", [128, QC], BF16))

        pbig = en(nc.psum_tensor("pbig", [128, 2048], F32))
        pav = en(nc.psum_tensor("pav", [128, 1024], F32))
        pgen = en(nc.psum_tensor("pgen", [128, 1024], F32))

        P = B()
        T = Sem(en(nc.semaphore("T")))
        SS = Sem(en(nc.semaphore("SS")))
        V = Sem(en(nc.semaphore("V")))
        G = Sem(en(nc.semaphore("G")))
        d_x = [Sem(en(nc.semaphore(f"d_x{i}"))) for i in range(2)]
        d_sp = [Sem(en(nc.semaphore(f"d_sp{i}"))) for i in range(2)]
        d_k = [Sem(en(nc.semaphore(f"d_k{i}"))) for i in range(2)]
        d_v = [Sem(en(nc.semaphore(f"d_v{i}"))) for i in range(2)]
        d_gk = [Sem(en(nc.semaphore(f"d_gk{i}"))) for i in range(2)]
        d_gv = [Sem(en(nc.semaphore(f"d_gv{i}"))) for i in range(2)]
        d_w = Sem(en(nc.semaphore("d_w")))
        d_o = Sem(en(nc.semaphore("d_o")))
        d_n = Sem(en(nc.semaphore("d_n")))
        block = en(nc.Block())

        t_op = lambda fn: P.op("tensor", T, fn)
        s_op = lambda fn: P.op("scalar", SS, fn)
        v_op = lambda fn: P.op("vector", V, fn)
        g_op = lambda fn: P.op("gpsimd", G, fn)

        def hview(t_hbm, qc):
            return t_hbm[:, qc * QC:(qc + 1) * QC].rearrange("(r p) n -> p r n", p=128)

        def xt_like_xn1(qc):
            return xn1s[:, qc * QC:(qc + 1) * QC].rearrange("(r p) n -> p r n", p=128)

        def xv(i):
            return xb[i][:].rearrange("p (r n) -> p r n", r=NCH)

        def wslice(sb, kc, m):
            return sb[:, kc * C + m * 128: kc * C + (m + 1) * 128]

        def xn_c(r):
            return xn[:, r * (QC + 1) + 1: r * (QC + 1) + 1 + QC]

        def xn_p(r):
            return xn[:, r * (QC + 1): r * (QC + 1) + QC]

        r_mean = fr[0:1, 0 * QC:1 * QC]
        r_var = fr[0:1, 1 * QC:2 * QC]
        r_rstd = fr[0:1, 2 * QC:3 * QC]
        r_t = fr[0:1, 3 * QC:4 * QC]
        r_nmr = r_mean
        b_rhi = br[0:1, 0 * QC:1 * QC]
        b_rlo = br[0:1, 1 * QC:2 * QC]
        b_nhi = br[0:1, 2 * QC:3 * QC]
        b_nlo = br[0:1, 3 * QC:4 * QC]
        u_y = u64[64:65, 0:QC]
        u_t = u64[64:65, QC:2 * QC]
        u_hi = ub64[64:65, 0:QC]
        u_lo = ub64[64:65, QC:2 * QC]
        odst = u64[:].bitcast(BF16)[:, 0:QC]
        qbufs = (q_sb, stgk[0])

        P.dma("sync", d_x[0], lambda e: e.dma_start(out=xv(0), in_=hview(xt, 0)))
        P.dma("sync", d_x[1], lambda e: e.dma_start(out=xv(1), in_=hview(xt, 1)))
        P.dma("sync", d_w, lambda e: e.dma_start(out=bout_sb[:], in_=bout))
        P.dma("sync", d_w, lambda e: e.dma_start(out=mur_sb[:], in_=murh))
        P.dma("sync", d_w, lambda e: e.dma_start(out=muk_sb[:], in_=mukh))

        def wv_(sb):
            return sb[:].rearrange("p (kc d) -> p kc d", kc=NCH)

        def wh_(t_hbm):
            return t_hbm.rearrange("(kc p) d -> p kc d", p=128)

        for sb_t, hb in ((wr_sb, wrt), (wk_sb, wkt), (wv_sb, wvt), (wout_sb, woutt)):
            P.dma("gpsimd", d_w, lambda e, sb_t=sb_t, hb=hb: e.dma_start(
                out=wv_(sb_t), in_=wh_(hb)))
        for sb_t, hb in ((wqa_sb, wqa), (wka_sb, wka), (wva_sb, wva)):
            P.dma("gpsimd", d_w, lambda e, sb_t=sb_t, hb=hb: e.dma_start(
                out=sb_t[:], in_=hb))
        w_tick = d_w.n

        g_op(lambda e: e.memset(ones_sb[:], 1.0))
        g_op(lambda e: e.memset(bnd1_sb[:], 0.0))
        g_op(lambda e: e.memset(bnd2_sb[:], 0.0))
        g_op(lambda e: e.memset(qmax_sb[:], 0.0))
        g_op(lambda e: e.memset(kmax_sb[:], 0.0))
        g_op(lambda e: e.memset(stgv[0][:], 1.0))
        init_tick = g_op(lambda e: e.memset(stgv[1][:], 1.0))

        P.wait("vector", G, init_tick)
        P.wait("tensor", G, init_tick)
        P.wait("tensor", d_w, w_tick)
        P.wait("scalar", G, init_tick)

        def newton_rsqrt(vrow, yrow, trow):
            irow = trow.bitcast(I32)
            v_op(lambda e: e.tensor_single_scalar(irow, vrow.bitcast(I32), 1,
                                                  ALU.logical_shift_right))
            v_op(lambda e: e.tensor_scalar(irow, irow, -1, MAGIC_RSQRT, ALU.mult, ALU.add))
            v_op(lambda e: e.tensor_copy(yrow, irow.bitcast(F32)))
            for _ in range(3):
                v_op(lambda e: e.tensor_tensor(trow, yrow, yrow, ALU.mult))
                v_op(lambda e: e.tensor_tensor(trow, trow, vrow, ALU.mult))
                v_op(lambda e: e.tensor_scalar(trow, trow, -0.5, 1.5, ALU.mult, ALU.add))
                v_op(lambda e: e.tensor_tensor(yrow, yrow, trow, ALU.mult))

        def newton_recip(vrow, yrow, trow):
            irow = trow.bitcast(I32)
            v_op(lambda e: e.tensor_scalar(irow, vrow.bitcast(I32), -1, MAGIC_RECIP,
                                           ALU.mult, ALU.add))
            v_op(lambda e: e.tensor_copy(yrow, irow.bitcast(F32)))
            for _ in range(3):
                v_op(lambda e: e.tensor_tensor(trow, vrow, yrow, ALU.mult))
                v_op(lambda e: e.tensor_scalar(trow, trow, -1.0, 2.0, ALU.mult, ALU.add))
                v_op(lambda e: e.tensor_tensor(yrow, yrow, trow, ALU.mult))

        def ln_stat_chunk(sv, r, deps=()):
            cb = cast[r % 2]
            for (sem, tick) in deps:
                P.wait("scalar", sem, tick)
            P.wf("scalar", f"cast{r % 2}")
            if r == 0:
                P.wf("tensor", "pg6")
                P.wf("tensor", "pg7")
            s_op(lambda e, r=r, cb=cb: e.activation(
                cb[:, 0:QC], sv[:, r, :], AF.Copy, bias=0.0, scale=1.0))
            st = s_op(lambda e, r=r, cb=cb: e.activation(
                cb[:, QC:2 * QC], sv[:, r, :], AF.Square, bias=0.0, scale=1.0))
            P.wait("tensor", SS, st)
            t_op(lambda e, cb=cb, r=r: e.matmul(
                pgen[0:1, 0:QC], ones_sb[:, 0:1], cb[:, 0:QC],
                start=(r == 0), stop=(r == NCH - 1)))
            m2 = t_op(lambda e, cb=cb, r=r: e.matmul(
                pgen[0:1, QC:2 * QC], ones_sb[:, 0:1], cb[:, QC:2 * QC],
                start=(r == 0), stop=(r == NCH - 1)))
            P.sf(f"cast{r % 2}", T, m2)
            return m2

        def ln_rows(last_mm):
            P.wait("vector", T, last_mm)
            v_op(lambda e: e.tensor_single_scalar(r_mean, pgen[0:1, 0:QC], 1.0 / C, ALU.mult))
            v_op(lambda e: e.tensor_single_scalar(r_var, pgen[0:1, QC:2 * QC], 1.0 / C, ALU.mult))
            v_op(lambda e: e.tensor_tensor(r_t, r_mean, r_mean, ALU.mult))
            v_op(lambda e: e.tensor_tensor(r_var, r_var, r_t, ALU.subtract))
            v_op(lambda e: e.tensor_scalar_add(r_var, r_var, EPS))
            newton_rsqrt(r_var, r_rstd, r_t)
            v_op(lambda e: e.scalar_tensor_tensor(r_nmr, r_mean, -1.0, r_rstd,
                                                  ALU.mult, ALU.mult))
            v_op(lambda e: e.tensor_copy(b_rhi, r_rstd))
            v_op(lambda e: e.tensor_tensor(b_rlo, r_rstd, b_rhi, ALU.subtract))
            v_op(lambda e: e.tensor_copy(b_nhi, r_nmr))
            return v_op(lambda e: e.tensor_tensor(b_nlo, r_nmr, b_nhi, ALU.subtract))

        def ln_reps(hl, target):
            if target == 'pgen':
                rsl, nsl = pgen[:, 0:QC], pgen[:, QC:2 * QC]
                k1, k2 = "pg6", "pg7"
            else:
                rsl, nsl = pav[:, 0:QC], pav[:, QC:2 * QC]
                k1, k2 = "pavH0", "pavH1"
                P.wf("tensor", k1)
                P.wf("tensor", k2)
            P.wait("tensor", V, hl)
            t_op(lambda e: e.matmul(rsl, ones_sb[0:1, :], b_rhi, start=True, stop=False))
            t_op(lambda e: e.matmul(rsl, ones_sb[0:1, :], b_rlo, start=False, stop=True))
            t_op(lambda e: e.matmul(nsl, ones_sb[0:1, :], b_nhi, start=True, stop=False))
            rep = t_op(lambda e: e.matmul(nsl, ones_sb[0:1, :], b_nlo, start=False, stop=True))
            return rep, rsl, nsl, k1, k2

        def ln_norm(sv, dst_fn, rep, rsl, nsl, k1, k2, pre=()):
            P.wait("vector", T, rep)
            for (sem, tick) in pre:
                P.wait("vector", sem, tick)
            last = 0
            for r in range(NCH):
                v_op(lambda e, r=r: e.tensor_tensor(
                    t0[:], sv[:, r, :], rsl, ALU.mult))
                last = v_op(lambda e, r=r: e.tensor_tensor(
                    dst_fn(r), t0[:], nsl, ALU.add))
            P.sf(k1, V, last)
            P.sf(k2, V, last)
            return last

        def ln(src, dst_fn, deps, target='pgen', pre=()):
            sv = src[:].rearrange("p (r n) -> p r n", r=NCH)
            last_mm = 0
            for r in range(NCH):
                last_mm = ln_stat_chunk(sv, r, deps if r == 0 else ())
            hl = ln_rows(last_mm)
            if target != 'pgen':
                P.sf("pg6", V, hl)
                P.sf("pg7", V, hl)
            rep, rsl, nsl, k1, k2 = ln_reps(hl, target)
            return ln_norm(sv, dst_fn, rep, rsl, nsl, k1, k2, pre)

        def boundary_restore(bnd):
            last = 0
            for r in range(NCH):
                last = v_op(lambda e, r=r, bnd=bnd: e.tensor_copy(
                    xn[:, r * (QC + 1): r * (QC + 1) + 1], bnd[:, r:r + 1]))
            return last

        def ffn(src, bnd):
            sv = src[:].rearrange("p (r n) -> p r n", r=NCH)
            b1v = b1[:].rearrange("p (r n) -> p r n", r=NCH)
            b2v = b2[:].rearrange("p (r n) -> p r n", r=NCH)
            xr_t = 0
            for r in range(NCH):
                if r == 0:
                    P.wf("vector", "b1")
                    P.wf("gpsimd", "b2")
                dt_ = v_op(lambda e, r=r: e.tensor_tensor(b1v[:, r, :], xn_c(r), xn_p(r),
                                                          ALU.subtract))
                if r == 0:
                    P.wait("gpsimd", V, dt_)
                g_op(lambda e, r=r: e.tensor_tensor(b2v[:, r, :], xn_c(r), xn_p(r),
                                                    ALU.subtract))
                g_op(lambda e, r=r: e.tensor_tensor(
                    b2v[:, r, :], b2v[:, r, :],
                    mur_sb[:, r:r + 1].broadcast_to([128, QC]), ALU.mult))
                xr_t = g_op(lambda e, r=r: e.tensor_tensor(
                    b2v[:, r, :], b2v[:, r, :], xn_p(r), ALU.add))
                lerp_t = v_op(lambda e, r=r: e.scalar_tensor_tensor(
                    b1v[:, r, :], b1v[:, r, :], muk_sb[:, r:r + 1], xn_p(r),
                    ALU.mult, ALU.add))
            bsave = 0
            for r in range(NCH):
                bsave = v_op(lambda e, r=r, bnd=bnd: e.tensor_copy(
                    bnd[:, r:r + 1], xn[:, r * (QC + 1) + QC: r * (QC + 1) + QC + 1]))
            P.wait("tensor", V, lerp_t)
            kact_first = True
            kact_t = 0
            for m in range(NCH):
                bk = m % 4
                pb = pbig[:, bk * QC:(bk + 1) * QC]
                P.wf("tensor", f"pb{bk}")
                mt = 0
                for kc in range(NCH):
                    mt = t_op(lambda e, m=m, kc=kc, pb=pb: e.matmul(
                        pb, wslice(wk_sb, kc, m), b1v[:, kc, :],
                        start=(kc == 0), stop=(kc == NCH - 1)))
                tb = tb_[m % 2]
                P.wait("scalar", T, mt)
                P.wf("scalar", f"tb{m % 2}")
                rl = s_op(lambda e, pb=pb, tb=tb: e.activation(
                    tb[:], pb, AF.Relu, bias=0.0, scale=1.0))
                P.sf(f"pb{bk}", SS, rl)
                P.wait("vector", SS, rl)
                if kact_first:
                    P.wait("vector", G, xr_t)
                    kact_first = False
                kact_t = v_op(lambda e, m=m, tb=tb: e.tensor_tensor(
                    xn_c(m), tb[:], tb[:], ALU.mult))
                P.sf(f"tb{m % 2}", V, kact_t)
            out_t = 0
            out_ts = []
            for m in range(NCH):
                bkr = m % 2
                bkv = 2 + m % 2
                pr = pbig[:, bkr * QC:(bkr + 1) * QC]
                pv = pbig[:, bkv * QC:(bkv + 1) * QC]
                P.wf("tensor", f"pb{bkr}")
                if m == 0:
                    P.wait("tensor", G, xr_t)
                mt = 0
                for kc in range(NCH):
                    mt = t_op(lambda e, m=m, kc=kc, pr=pr: e.matmul(
                        pr, wslice(wr_sb, kc, m), b2v[:, kc, :],
                        start=(kc == 0), stop=(kc == NCH - 1)))
                tb = tb_[m % 2]
                P.wait("scalar", T, mt)
                P.wf("scalar", f"tb{m % 2}")
                th = s_op(lambda e, pr=pr, tb=tb: e.activation(
                    tb[:], pr, AF.Tanh, bias=0.0, scale=0.5))
                P.sf(f"pb{bkr}", SS, th)
                P.wf("tensor", f"pb{bkv}")
                if m == 0:
                    P.wait("tensor", V, kact_t)
                mtv = 0
                for kc in range(NCH):
                    mtv = t_op(lambda e, m=m, kc=kc, pv=pv: e.matmul(
                        pv, wslice(wv_sb, kc, m), xn_c(kc),
                        start=(kc == 0), stop=(kc == NCH - 1)))
                P.wait("vector", SS, th)
                P.wait("vector", T, mtv)
                ct = v_op(lambda e, tb=tb, pv=pv: e.scalar_tensor_tensor(
                    t0[:], tb[:], 1.0, pv, ALU.add, ALU.mult))
                P.sf(f"tb{m % 2}", V, ct)
                P.sf(f"pb{bkv}", V, ct)
                out_t = v_op(lambda e, m=m, sv=sv: e.scalar_tensor_tensor(
                    sv[:, m, :], t0[:], 0.25, sv[:, m, :], ALU.mult, ALU.add))
                out_ts.append(out_t)
            P.sf("b1", V, out_t)
            P.sf("b2", V, out_t)
            P.sf("xnreg", T, mtv)
            return out_t, out_ts

        spill_tick = [0, 0]
        a_n2t = [0, 0]

        def ln1_emit(qc_):
            b_ = qc_ % 2
            pre = [P.free["xnreg"]] if "xnreg" in P.free else []
            return ln(xb[b_], xn_c, [(d_x[b_], 16 * (qc_ // 2 + 1))],
                      target='pav', pre=pre)

        ln1_emit(0)
        for qc in range(nqc):
            buf = qc % 2
            src = xb[buf]
            boundary_restore(bnd1_sb)
            ffn_t, ffn_ts = ffn(src, bnd1_sb)
            P.wait("scalar", V, ffn_t)
            sp = P.dma("scalar", d_sp[buf], lambda e, qc=qc, src=src: e.dma_start(
                out=hview(x1s, qc), in_=src[:].rearrange("p (r n) -> p r n", r=NCH)))
            spill_tick[buf] = sp
            if stage == 'ffn':
                continue
            xn1v = xn1[:].rearrange("p (r n) -> p r n", r=NCH)
            sv2 = src[:].rearrange("p (r n) -> p r n", r=NCH)
            lm2 = 0
            for m in range(NCH):
                lm2 = ln_stat_chunk(sv2, m, deps=[(V, ffn_ts[m])])
            hl2_ = ln_rows(lm2)
            rep2, rsl2, nsl2, k12, k22 = ln_reps(hl2_, 'pgen')
            P.wf("vector", "xn1")
            if qc > 0:
                P.wait("vector", d_n, d_n.n)
            n2t = ln_norm(sv2, lambda r: xn1v[:, r, :], rep2, rsl2, nsl2, k12, k22)
            a_n2t[buf] = n2t
            if qc + 1 < nqc:
                ln1_emit(qc + 1)
            P.wait("scalar", V, n2t)
            xn1_spill = P.dma("scalar", d_n, lambda e, qc=qc: e.dma_start(
                out=xt_like_xn1(qc), in_=xn1[:].rearrange("p (r n) -> p r n", r=NCH)))
            last_q_mm = 0

            def kv_tail(c):
                p_, kb, qb, kmt_, qmt_ = c
                pairch = xn1v[:, p_, :]
                sg = p_ % 2
                kslice = pav[:, kb * QC:(kb + 1) * QC]
                qslice = pbig[:, qb * QC:(qb + 1) * QC]
                P.wait("scalar", T, kmt_)
                P.wait("scalar", d_gk[0], d_gk[0].n)
                P.wait("scalar", d_gk[1], d_gk[1].n)
                ke = s_op(lambda e, sg=sg, kslice=kslice: e.activation(
                    stgk[sg][:], kslice, AF.Copy, bias=0.0, scale=1.0))
                P.wait("scalar", SS, ke)
                P.dma("scalar", d_gk[sg], lambda e, p_=p_, qc=qc, sg=sg: e.dma_start(
                    out=ks[p_, :, qc * QC:(qc + 1) * QC], in_=stgk[sg][:]))
                P.wf("scalar", "cast0")
                ksq = s_op(lambda e, kslice=kslice: e.activation(
                    cast[0][:, QC:2 * QC], kslice, AF.Square, bias=0.0, scale=1.0))
                P.sf(f"pavH{kb}", SS, ksq)
                P.wait("tensor", SS, ksq)
                P.wf("tensor", "pg6")
                ksm = t_op(lambda e: e.matmul(pgen[0:1, 0:QC], ones_sb[:, 0:1],
                                              cast[0][:, QC:2 * QC], start=True, stop=True))
                P.sf("cast0", T, ksm)
                P.wait("vector", T, ksm)
                krd = v_op(lambda e: e.reduce_max(r_t[0:1, 0:1], pgen[0:1, 0:QC], axis=AX))
                P.wait("vector", V, krd)
                kred = v_op(lambda e, p_=p_: e.tensor_tensor(
                    kmax_sb[0:1, p_:p_ + 1], kmax_sb[0:1, p_:p_ + 1], r_t[0:1, 0:1], ALU.max))
                P.sf("pg6", V, kred)
                P.wait("scalar", T, qmt_)
                P.wf("scalar", "cast1")
                qsq = s_op(lambda e, qslice=qslice: e.activation(
                    cast[1][:, QC:2 * QC], qslice, AF.Square, bias=0.0, scale=INV32))
                P.sf(f"pb{qb}", SS, qsq)
                P.wait("tensor", SS, qsq)
                P.wf("tensor", "pg7")
                qsm = t_op(lambda e: e.matmul(pgen[0:1, QC:2 * QC], ones_sb[:, 0:1],
                                              cast[1][:, QC:2 * QC], start=True, stop=True))
                P.sf("cast1", T, qsm)
                P.wait("vector", T, qsm)
                qrd = v_op(lambda e: e.reduce_max(r_t[0:1, 1:2], pgen[0:1, QC:2 * QC], axis=AX))
                P.wait("vector", V, qrd)
                qred = v_op(lambda e, p_=p_: e.tensor_tensor(
                    qmax_sb[0:1, p_:p_ + 1], qmax_sb[0:1, p_:p_ + 1], r_t[0:1, 1:2], ALU.max))
                P.sf("pg7", V, qred)
                P.wf("tensor", "pb0")
                P.wf("tensor", "pb1")
                vmt_ = 0
                for tc in range(4):
                    lo = tc * 128
                    P.raw("tensor", lambda e, pairch=pairch, lo=lo, tc=tc: e.matmul(
                        pbig[:, tc * 64: tc * 64 + 64], pairch[0:64, lo:lo + 128],
                        wva_sb[0:64, :], start=True, stop=True, tile_position=(0, 0)))
                    vmt_ = t_op(lambda e, pairch=pairch, lo=lo, tc=tc: e.matmul(
                        pbig[:, QC + tc * 64: QC + tc * 64 + 64], pairch[64:128, lo:lo + 128],
                        wva_sb[64:128, :], start=True, stop=True, tile_position=(64, 0)))
                sgv = p_ % 2
                P.wait("vector", T, vmt_)
                P.wait("vector", d_gv[0], d_gv[0].n)
                P.wait("vector", d_gv[1], d_gv[1].n)
                stv = stgv[sgv][:].rearrange("q (t c) -> q t c", c=130)
                pa = pbig[:, 0:256].rearrange("q (t c) -> q t c", c=64)
                pb_ = pbig[:, QC:QC + 256].rearrange("q (t c) -> q t c", c=64)
                v_op(lambda e, stv=stv, pa=pa: e.tensor_copy(stv[:, :, 0:64], pa))
                vev = v_op(lambda e, stv=stv, pb_=pb_: e.tensor_copy(stv[:, :, 65:129], pb_))
                P.sf("pb0", V, vev)
                P.sf("pb1", V, vev)
                P.wait("scalar", V, vev)
                P.dma("scalar", d_gv[sgv], lambda e, p_=p_, qc=qc, sgv=sgv: e.dma_start(
                    out=vs[p_, :, qc * 520:(qc + 1) * 520], in_=stgv[sgv][:]))
                return vmt_

            prevkv = None
            for p in range(NPAIR):
                pairch = xn1v[:, p, :]
                kb = p % 2
                qb = 2 + p % 2
                P.wf("tensor", f"pavH{kb}")
                if p == 0:
                    P.wait("tensor", V, n2t)
                P.raw("tensor", lambda e, pairch=pairch, kb=kb: e.matmul(
                    pav[0:64, kb * QC:kb * QC + QC], wka_sb[0:64, :], pairch[0:64, :],
                    start=True, stop=True, tile_position=(0, 0)))
                kmt = t_op(lambda e, pairch=pairch, kb=kb: e.matmul(
                    pav[64:128, kb * QC:kb * QC + QC], wka_sb[64:128, :], pairch[64:128, :],
                    start=True, stop=True, tile_position=(64, 64)))
                P.wf("tensor", f"pb{qb}")
                P.raw("tensor", lambda e, pairch=pairch, qb=qb: e.matmul(
                    pbig[0:64, qb * QC:qb * QC + QC], wqa_sb[0:64, :], pairch[0:64, :],
                    start=True, stop=True, tile_position=(0, 0)))
                qmt = t_op(lambda e, pairch=pairch, qb=qb: e.matmul(
                    pbig[64:128, qb * QC:qb * QC + QC], wqa_sb[64:128, :], pairch[64:128, :],
                    start=True, stop=True, tile_position=(64, 64)))
                last_q_mm = qmt
                if prevkv is not None:
                    kv_tail(prevkv)
                prevkv = (p, kb, qb, kmt, qmt)
            last_vmt = kv_tail(prevkv)
            last_q_mm = max(last_q_mm, last_vmt)
            P.sf("xn1", T, last_q_mm)
            if qc + 2 < nqc:
                P.wait("sync", d_sp[buf], spill_tick[buf])
                P.wait("sync", V, n2t)
                P.dma("sync", d_x[buf], lambda e, qc=qc, buf=buf: e.dma_start(
                    out=xv(buf), in_=hview(xt, qc + 2)))

        t1_ = v_op(lambda e: e.tensor_tensor(r_t[0:1, 0:NPAIR], qmax_sb[0:1, 0:NPAIR],
                                             kmax_sb[0:1, 0:NPAIR], ALU.mult))
        P.wait("vector", V, t1_)
        t2_ = v_op(lambda e: e.tensor_scalar(r_t[0:1, 0:NPAIR], r_t[0:1, 0:NPAIR], 0.51, 1.0,
                                             ALU.mult, ALU.add))
        P.wait("vector", V, t2_)
        t3_ = v_op(lambda e: e.tensor_copy(br[0:1, 0:NPAIR], r_t[0:1, 0:NPAIR]))
        P.wait("vector", V, t3_)
        chl = v_op(lambda e: e.tensor_tensor(br[0:1, NPAIR:2 * NPAIR],
                                             r_t[0:1, 0:NPAIR], br[0:1, 0:NPAIR],
                                             ALU.subtract))
        P.wf("tensor", "pg6")
        P.wait("tensor", V, chl)
        t_op(lambda e: e.matmul(pgen[:, 0:NPAIR], ones_sb[0:1, :], br[0:1, 0:NPAIR],
                                start=True, stop=False))
        cm = t_op(lambda e: e.matmul(pgen[:, 0:NPAIR], ones_sb[0:1, :],
                                     br[0:1, NPAIR:2 * NPAIR], start=False, stop=True))
        P.wait("scalar", T, cm)
        negc_t = s_op(lambda e: e.activation(negc_sb[:], pgen[:, 0:NPAIR], AF.Copy,
                                             bias=0.0, scale=-1.0))
        P.sf("pg6", SS, negc_t)
        if debug:
            P.wait("scalar", SS, negc_t)
            P.dma("scalar", d_w, lambda e: e.dma_start(out=dbgn, in_=negc_sb[:]))
            P.dma("scalar", d_w, lambda e: e.dma_start(out=dbgs[0:1, :], in_=qmax_sb[:]))
            P.dma("scalar", d_w, lambda e: e.dma_start(out=dbgs[1:2, :], in_=kmax_sb[:]))

        if not do_b:
            P.wait("sync", d_sp[0], d_sp[0].n)
            P.wait("sync", d_sp[1], d_sp[1].n)
            for s_ in (d_gk[0], d_gk[1], d_gv[0], d_gv[1]):
                P.wait("sync", s_, s_.n)
            P.run(block)
            return nc
        P.wait("sync", d_sp[0], spill_tick[0])
        P.wait("sync", V, a_n2t[0])
        ld0 = P.dma("sync", d_x[0], lambda e: e.dma_start(out=xv(0), in_=hview(x1s, 0)))
        P.wait("sync", d_sp[1], spill_tick[1])
        P.wait("sync", V, a_n2t[1])
        ld1 = P.dma("sync", d_x[1], lambda e: e.dma_start(out=xv(1), in_=hview(x1s, 1)))
        x_load_tick = [ld0, ld1]
        for s_ in (d_gk[0], d_gk[1], d_gv[0], d_gv[1]):
            P.wait("sync", s_, s_.n)
        kv_tick = {}
        kv_tick[0] = (P.dma("sync", d_k[0], lambda e: e.dma_start(out=kp[0][:], in_=ks[0])),
                      P.dma("sync", d_v[0], lambda e: e.dma_start(out=vp[0][:], in_=vs[0])))
        kv_tick[1] = (P.dma("sync", d_k[1], lambda e: e.dma_start(out=kp[1][:], in_=ks[1])),
                      P.dma("sync", d_v[1], lambda e: e.dma_start(out=vp[1][:], in_=vs[1])))

        b2v = b2[:].rearrange("p (r n) -> p r n", r=NCH)
        pt_guard = [0, 0]

        def norm_rows(ab2):
            hl2 = 0
            for hh in range(2):
                vrow = avout[ab2][64:65, hh * QC:(hh + 1) * QC]
                newton_recip(vrow, u_y, u_t)
                v_op(lambda e, hh=hh: e.tensor_copy(
                    ub64[64:65, (2 * hh) * QC:(2 * hh + 1) * QC], u_y))
                hl2 = v_op(lambda e, hh=hh: e.tensor_tensor(
                    ub64[64:65, (2 * hh + 1) * QC:(2 * hh + 2) * QC], u_y,
                    ub64[64:65, (2 * hh) * QC:(2 * hh + 1) * QC], ALU.subtract))
            return hl2

        def norm_apply(pp, ab2, hl2):
            last = 0
            for hh in range(2):
                P.wf("tensor", "pg7")
                P.wait("tensor", V, hl2)
                P.raw("tensor", lambda e, hh=hh: e.matmul(
                    pgen[0:64, QC:2 * QC], ones_sb[64:65, 0:64],
                    ub64[64:65, (2 * hh) * QC:(2 * hh + 1) * QC],
                    start=True, stop=False, tile_position=(64, 0)))
                rpt = t_op(lambda e, hh=hh: e.matmul(
                    pgen[0:64, QC:2 * QC], ones_sb[64:65, 0:64],
                    ub64[64:65, (2 * hh + 1) * QC:(2 * hh + 2) * QC],
                    start=False, stop=True, tile_position=(64, 0)))
                P.wait("vector", T, rpt)
                if hh == 0:
                    last = v_op(lambda e, pp=pp, ab2=ab2: e.tensor_tensor(
                        b2v[0:64, pp, :], avout[ab2][0:64, 0:QC],
                        pgen[0:64, QC:2 * QC], ALU.mult))
                else:
                    P.wf("vector", "odst")
                    last = v_op(lambda e, ab2=ab2: e.tensor_tensor(
                        odst[0:64, :], avout[ab2][0:64, QC:2 * QC],
                        pgen[0:64, QC:2 * QC], ALU.mult))
                    P.wait("scalar", V, last)
                    ot = P.dma("scalar", d_o, lambda e, pp=pp: e.dma_start(
                        out=b2[64:128, pp * QC:(pp + 1) * QC], in_=odst[0:64, :]))
                    P.sf("odst", d_o, ot)
                P.sf("pg7", V, last)
            P.sf(f"avout{ab2}", V, last)
            return last

        xn1v = xn1[:].rearrange("p (r n) -> p r n", r=NCH)
        P.wait("sync", d_n, d_n.n)
        P.wait("sync", T, P.free["xn1"][1] if P.free.get("xn1") and P.free["xn1"][0] is T else 0)
        xn1_load_tick = P.dma("sync", d_n, lambda e: e.dma_start(
            out=xn1[:].rearrange("p (r n) -> p r n", r=NCH), in_=xt_like_xn1(0)))
        P.sf("qb1", d_gk[0], d_gk[0].n)
        for qc in range(nqc):
            buf = qc % 2
            src = xb[buf]
            n2t = 0
            prev_pair = None
            last_q_mm = 0
            norm_t = 0

            def qproj(p_):
                P.wf("tensor", "pg6")
                if p_ == 0:
                    P.wait("tensor", d_n, xn1_load_tick)
                pairch = xn1v[:, p_, :]
                P.raw("tensor", lambda e, pairch=pairch: e.matmul(
                    pgen[0:64, 0:QC], wqa_sb[0:64, :], pairch[0:64, :],
                    start=True, stop=True, tile_position=(0, 0)))
                qmt_ = t_op(lambda e, pairch=pairch: e.matmul(
                    pgen[64:128, 0:QC], wqa_sb[64:128, :], pairch[64:128, :],
                    start=True, stop=True, tile_position=(64, 64)))
                qb_ = qbufs[p_ % 2]
                P.wait("scalar", T, qmt_)
                P.wf("scalar", f"qb{p_ % 2}")
                qe_ = s_op(lambda e, qb_=qb_: e.activation(qb_[:, 0:QC], pgen[:, 0:QC],
                                                           AF.Copy, bias=0.0, scale=INV32))
                P.sf("pg6", SS, qe_)
                return qmt_, qe_

            q_next = qproj(0)
            for p in range(NPAIR):
                gidx = qc * NPAIR + p
                pbuf = gidx % 2
                qmt, qe = q_next
                last_q_mm = qmt
                q_sbv = qbufs[p % 2]
                if p + 1 < NPAIR:
                    q_next = qproj(p + 1)
                if stage == 'bq':
                    P.sf("q_sb", SS, qe)
                    continue
                kw, vw = kv_tick[gidx]
                exp_ticks = [0] * KCN
                exp_av = [0] * KCN

                def emit_av(kc):
                    sb2 = kc % 2
                    if kc == 0:
                        P.wf("tensor", "pavB")
                        P.wait("tensor", d_v[pbuf], vw)
                    P.wait("tensor", SS, exp_ticks[kc])
                    P.raw("tensor", lambda e, kc=kc, sb2=sb2, pbuf=pbuf: e.matmul(
                        pav[0:65, 0:QC], vp[pbuf][:, kc * 130: kc * 130 + 65],
                        pt[sb2][:, 0:QC], start=(kc == 0), stop=(kc == KCN - 1)))
                    t_ = t_op(lambda e, kc=kc, sb2=sb2, pbuf=pbuf: e.matmul(
                        pav[0:65, QC:2 * QC], vp[pbuf][:, kc * 130 + 65: kc * 130 + 130],
                        pt[sb2][:, QC:2 * QC], start=(kc == 0), stop=(kc == KCN - 1)))
                    exp_av[kc] = t_
                    return t_

                def emit_sc(kc):
                    sb_ = kc % 2
                    if kc < 2:
                        P.wf("tensor", f"pbB{sb_}")
                        P.wf("tensor", f"pb{sb_ * 2}")
                        P.wf("tensor", f"pb{sb_ * 2 + 1}")
                        if kc == 0:
                            P.wait("tensor", SS, qe)
                            P.wait("tensor", d_k[pbuf], kw)
                    else:
                        P.wait("tensor", SS, exp_ticks[kc - 2])
                    P.raw("tensor", lambda e, kc=kc, sb_=sb_, pbuf=pbuf, q_sbv=q_sbv: e.matmul(
                        pbig[:, sb_ * 1024: sb_ * 1024 + QC],
                        kp[pbuf][0:64, kc * 128:(kc + 1) * 128],
                        q_sbv[0:64, 0:QC], start=True, stop=True, tile_position=(0, 0)))
                    return t_op(lambda e, kc=kc, sb_=sb_, pbuf=pbuf, q_sbv=q_sbv: e.matmul(
                        pbig[:, sb_ * 1024 + QC: sb_ * 1024 + 2 * QC],
                        kp[pbuf][64:128, kc * 128:(kc + 1) * 128],
                        q_sbv[64:128, 0:QC], start=True, stop=True, tile_position=(64, 0)))

                def emit_exp(kc, sct):
                    sb_ = kc % 2
                    P.wait("scalar", T, sct)
                    if kc >= 2:
                        P.wait("scalar", T, exp_av[kc - 2])
                    else:
                        P.wait("scalar", T, pt_guard[sb_])
                    exp_ticks[kc] = s_op(lambda e, sb_=sb_, p=p: e.activation(
                        pt[sb_][:], pbig[:, sb_ * 1024: sb_ * 1024 + 2 * QC],
                        AF.Exp, bias=negc_sb[:, p:p + 1], scale=1.0))

                for kc in range(KCN):
                    s0 = emit_sc(kc)
                    if kc >= 1:
                        emit_av(kc - 1)
                    emit_exp(kc, s0)
                av_last = emit_av(KCN - 1)
                P.sf(f"pbB0", SS, exp_ticks[KCN - 2])
                P.sf(f"pbB1", SS, exp_ticks[KCN - 1])
                pt_guard[0] = exp_av[KCN - 2]
                pt_guard[1] = exp_av[KCN - 1]
                P.sf(f"qb{p % 2}", T, av_last)
                ab = p % 2
                P.wait("vector", T, av_last)
                P.wf("vector", f"avout{ab}")
                ave = v_op(lambda e, ab=ab: e.tensor_copy(
                    avout[ab][0:65, 0:2 * QC], pav[0:65, 0:2 * QC]))
                P.sf("pavB", V, ave)
                if debug and qc == 0 and p == 0:
                    P.wait("scalar", V, ave)
                    P.dma("scalar", d_w, lambda e, ab=ab: e.dma_start(
                        out=dbga2, in_=avout[ab][:]))
                    P.wait("vector", d_w, d_w.n)

                if gidx + 2 < nqc * NPAIR:
                    npair = (gidx + 2) % NPAIR
                    P.wait("sync", T, av_last)
                    kt_ = P.dma("sync", d_k[pbuf], lambda e, npair=npair, pbuf=pbuf:
                                e.dma_start(out=kp[pbuf][:], in_=ks[npair]))
                    vt_ = P.dma("sync", d_v[pbuf], lambda e, npair=npair, pbuf=pbuf:
                                e.dma_start(out=vp[pbuf][:], in_=vs[npair]))
                    kv_tick[gidx + 2] = (kt_, vt_)
                if stage in ('bav',):
                    P.sf(f"avout{ab}", V, ave)
                    continue
                if prev_pair is not None:
                    norm_t = norm_apply(*prev_pair)
                prev_pair = (p, ab, norm_rows(ab))
            if stage not in ('bq', 'bsc', 'bav'):
                norm_t = norm_apply(*prev_pair)
            P.sf("xn1", T, last_q_mm)
            if qc + 1 < nqc:
                P.wait("sync", T, last_q_mm)
                xn1_load_tick = P.dma("sync", d_n, lambda e, qc=qc: e.dma_start(
                    out=xn1[:].rearrange("p (r n) -> p r n", r=NCH),
                    in_=xt_like_xn1(qc + 1)))
            if stage in ('bq', 'bsc', 'bav', 'bnorm'):
                continue
            if debug and qc == 0:
                P.wait("scalar", V, norm_t)
                P.wait("scalar", d_o, d_o.n)
                P.dma("scalar", d_w, lambda e: e.dma_start(out=dbga, in_=b2[:]))
                P.wait("tensor", d_w, d_w.n)
            P.wait("tensor", V, norm_t)
            P.wait("tensor", d_o, d_o.n)
            xsv = src[:].rearrange("p (r n) -> p r n", r=NCH)
            proj_t = 0
            last_mm3 = 0
            proj_ts = []
            for m in range(NCH):
                bk = m % 4
                pb = pbig[:, bk * QC:(bk + 1) * QC]
                P.wf("tensor", "pbB0" if bk < 2 else "pbB1")
                P.wf("tensor", f"pb{bk}")
                mt = 0
                for kc in range(NCH):
                    mt = t_op(lambda e, m=m, kc=kc, pb=pb: e.matmul(
                        pb, wslice(wout_sb, kc, m), b2v[:, kc, :],
                        start=(kc == 0), stop=(kc == NCH - 1)))
                tb = tb_[m % 2]
                P.wait("scalar", T, mt)
                P.wf("scalar", f"tb{m % 2}")
                pe = s_op(lambda e, pb=pb, tb=tb, m=m: e.activation(
                    tb[:], pb, AF.Identity, bias=bout_sb[:, m:m + 1], scale=1.0))
                P.sf(f"pb{bk}", SS, pe)
                if bk < 2:
                    P.sf("pbB0", SS, pe)
                else:
                    P.sf("pbB1", SS, pe)
                P.wait("vector", SS, pe)
                proj_t = v_op(lambda e, m=m, tb=tb, xsv=xsv: e.tensor_tensor(
                    xsv[:, m, :], xsv[:, m, :], tb[:], ALU.add))
                P.sf(f"tb{m % 2}", V, proj_t)
                proj_ts.append(proj_t)
            P.sf("b2", V, proj_t)
            for m in range(NCH):
                last_mm3 = ln_stat_chunk(xsv, m, deps=[(V, proj_ts[m])])
            if stage == 'bproj':
                P.wait("scalar", V, proj_t)
                yst = P.dma("scalar", d_sp[buf], lambda e, qc=qc, src=src: e.dma_start(
                    out=hview(yt, qc), in_=src[:].rearrange("p (r n) -> p r n", r=NCH)))
                if qc + 2 < nqc:
                    P.wait("sync", d_sp[buf], yst)
                    nt_ = P.dma("sync", d_x[buf], lambda e, qc=qc, buf=buf: e.dma_start(
                        out=xv(buf), in_=hview(x1s, qc + 2)))
                    x_load_tick[buf] = nt_
                continue
            hl3 = ln_rows(last_mm3)
            rep3, rsl3, nsl3, k13, k23 = ln_reps(hl3, 'pgen')
            sv3 = src[:].rearrange("p (r n) -> p r n", r=NCH)
            ln_norm(sv3, xn_c, rep3, rsl3, nsl3, k13, k23)
            boundary_restore(bnd2_sb)
            y_t, _yts = ffn(src, bnd2_sb)
            P.wait("scalar", V, y_t)
            yst = P.dma("scalar", d_sp[buf], lambda e, qc=qc, src=src: e.dma_start(
                out=hview(yt, qc), in_=src[:].rearrange("p (r n) -> p r n", r=NCH)))
            if qc + 2 < nqc:
                P.wait("sync", d_sp[buf], yst)
                nt_ = P.dma("sync", d_x[buf], lambda e, qc=qc, buf=buf: e.dma_start(
                    out=xv(buf), in_=hview(x1s, qc + 2)))
                x_load_tick[buf] = nt_

        P.wait("sync", d_sp[0], d_sp[0].n)
        P.wait("sync", d_sp[1], d_sp[1].n)

        P.run(block)

    return nc


_NC_CACHE = {}


def _get_nc():
    if "nc" not in _NC_CACHE:
        _NC_CACHE["nc"] = build_nc()
    return _NC_CACHE["nc"]


def make_in_maps(L0_attn_x, L0_attn_Wq, L0_attn_Wk, L0_attn_Wv, L0_attn_W_out,
                 L0_attn_b_out, L0_ffn_W_r, L0_ffn_W_k, L0_ffn_W_v, L0_ffn_mu_r,
                 L0_ffn_mu_k):
    x = np.asarray(L0_attn_x, dtype=np.float32)
    f32c = lambda a: np.ascontiguousarray(np.asarray(a, dtype=np.float32))

    def dualT(w):
        wt = f32c(np.asarray(w, dtype=np.float32).T)
        return np.ascontiguousarray(np.concatenate([wt, wt], axis=0))

    def chv(v):
        return np.ascontiguousarray(np.asarray(v, dtype=np.float32).reshape(8, 128).T)

    base = {
        "wrt": f32c(np.asarray(L0_ffn_W_r).T),
        "wkt": f32c(np.asarray(L0_ffn_W_k).T),
        "wvt": f32c(np.asarray(L0_ffn_W_v).T),
        "woutt": f32c(np.asarray(L0_attn_W_out).T),
        "wqa": dualT(L0_attn_Wq),
        "wka": dualT(L0_attn_Wk),
        "wva": dualT(L0_attn_Wv),
        "boutp": chv(L0_attn_b_out),
        "murh": chv(L0_ffn_mu_r),
        "mukh": chv(L0_ffn_mu_k),
    }
    in_maps = []
    for b in range(8):
        m = dict(base)
        m["xt"] = np.ascontiguousarray(x[b].T)
        in_maps.append(m)
    return in_maps


def kernel(**inputs):
    in_maps = make_in_maps(**inputs)
    nc = _get_nc()
    res = run_bass_kernel_spmd(nc, in_maps, core_ids=list(range(8)))
    out = np.empty((8, S, C), dtype=np.float32)
    for b in range(8):
        out[b] = res.results[b]["yt"].T
    return out


# version 21
# speedup vs baseline: 1.0274x; 1.0274x over previous
import numpy as np
from contextlib import ExitStack

import concourse.bass as bass
import concourse.mybir as mybir
from concourse.bass_utils import run_bass_kernel_spmd

F32 = mybir.dt.float32
BF16 = mybir.dt.bfloat16
I32 = mybir.dt.int32
AF = mybir.ActivationFunctionType
ALU = mybir.AluOpType
AX = mybir.AxisListType.X

C = 1024
S = 4096
NCH = 8
QC = 512
NQC = S // QC
D = 64
NPAIR = 8
KCN = 32
EPS = 1e-5
INV32 = 1.0 / 32.0

FINE_YIELDS = True
NO_OVERLAP = False
STRICT_KV = False
MAGIC_RSQRT = 0x5F3759DF
MAGIC_RECIP = 0x7EF477D5


class Sem:
    def __init__(self, h):
        self.h = h
        self.n = 0


class B:
    ENGS = ("sync", "tensor", "scalar", "vector", "gpsimd")

    def __init__(self):
        self.q = {e: [] for e in self.ENGS}
        self.free = {}

    def on(self, eng, fn):
        self.q[eng].append(fn)

    def wait(self, eng, sem, val):
        if val > 0:
            self.on(eng, lambda e, s=sem, v=val: e.wait_ge(s.h, v))

    def wf(self, eng, region):
        if region in self.free:
            sem, tick = self.free[region]
            self.wait(eng, sem, tick)

    def sf(self, region, sem, tick):
        self.free[region] = (sem, tick)

    def op(self, eng, sem, fn, inc=1):
        sem.n += inc
        self.on(eng, lambda e, f=fn, s=sem, i=inc: f(e).then_inc(s.h, i))
        return sem.n

    def raw(self, eng, fn):
        self.on(eng, fn)

    def dma(self, eng, sem, fn):
        sem.n += 16
        self.on(eng, lambda e, f=fn, s=sem: f(e).then_inc(s.h, 16))
        return sem.n

    def run(self, block):
        for eng in self.ENGS:
            fns = self.q[eng]
            def mk(fns):
                def body(e):
                    for f in fns:
                        f(e)
                return body
            getattr(block, eng)(mk(fns))


def build_nc(debug=False, nqc=NQC, do_b=True, stage='full'):
    nc = bass.Bass("TRN2", target_bir_lowering=False)

    xt = nc.dram_tensor("xt", [C, S], F32, kind="ExternalInput").ap()
    wrt = nc.dram_tensor("wrt", [C, C], F32, kind="ExternalInput").ap()
    wkt = nc.dram_tensor("wkt", [C, C], F32, kind="ExternalInput").ap()
    wvt = nc.dram_tensor("wvt", [C, C], F32, kind="ExternalInput").ap()
    woutt = nc.dram_tensor("woutt", [C, C], F32, kind="ExternalInput").ap()
    wqa = nc.dram_tensor("wqa", [128, D], F32, kind="ExternalInput").ap()
    wka = nc.dram_tensor("wka", [128, D], F32, kind="ExternalInput").ap()
    wva = nc.dram_tensor("wva", [128, D], F32, kind="ExternalInput").ap()
    bout = nc.dram_tensor("boutp", [128, NCH], F32, kind="ExternalInput").ap()
    murh = nc.dram_tensor("murh", [128, NCH], F32, kind="ExternalInput").ap()
    mukh = nc.dram_tensor("mukh", [128, NCH], F32, kind="ExternalInput").ap()
    yt = nc.dram_tensor("yt", [C, S], F32, kind="ExternalOutput").ap()
    scratch_kind = "ExternalOutput" if debug else "Internal"
    x1s = nc.dram_tensor("x1s", [C, S], F32, kind=scratch_kind).ap()
    dbgn = nc.dram_tensor("dbgn", [128, NPAIR], F32, kind=scratch_kind).ap()
    dbgr = nc.dram_tensor("dbgr", [8, QC], F32, kind=scratch_kind).ap()
    dbga = nc.dram_tensor("dbga", [128, S], BF16, kind=scratch_kind).ap()
    dbga2 = nc.dram_tensor("dbga2", [128, 1024], F32, kind=scratch_kind).ap()
    dbgp = nc.dram_tensor("dbgp", [128, KCN * 1024], BF16, kind=scratch_kind).ap()
    dbgs = nc.dram_tensor("dbgs", [2, 16], F32, kind=scratch_kind).ap()
    ks = nc.dram_tensor("ks", [NPAIR, 128, S], BF16, kind=scratch_kind).ap()
    vs = nc.dram_tensor("vs", [NPAIR, 128, KCN * 130], BF16, kind=scratch_kind).ap()
    xn1s = nc.dram_tensor("xn1s", [C, S], BF16, kind=scratch_kind).ap()

    with ExitStack() as ctx:
        en = ctx.enter_context
        wr_sb = en(nc.sbuf_tensor("wr_sb", [128, NCH * C], BF16))
        wk_sb = en(nc.sbuf_tensor("wk_sb", [128, NCH * C], BF16))
        wv_sb = en(nc.sbuf_tensor("wv_sb", [128, NCH * C], BF16))
        wout_sb = en(nc.sbuf_tensor("wout_sb", [128, NCH * C], BF16))
        wqa_sb = en(nc.sbuf_tensor("wqa_sb", [128, D], BF16))
        wka_sb = en(nc.sbuf_tensor("wka_sb", [128, D], BF16))
        wva_sb = en(nc.sbuf_tensor("wva_sb", [128, D], BF16))
        bout_sb = en(nc.sbuf_tensor("bout_sb", [128, NCH], F32))
        mur_sb = en(nc.sbuf_tensor("mur_sb", [128, NCH], F32))
        muk_sb = en(nc.sbuf_tensor("muk_sb", [128, NCH], F32))
        ones_sb = en(nc.sbuf_tensor("ones_sb", [128, 128], BF16))
        negc_sb = en(nc.sbuf_tensor("negc_sb", [128, NPAIR], F32))
        bnd1_sb = en(nc.sbuf_tensor("bnd1_sb", [128, NCH], BF16))
        bnd2_sb = en(nc.sbuf_tensor("bnd2_sb", [128, NCH], BF16))
        qmax_sb = en(nc.sbuf_tensor("qmax_sb", [1, 16], F32))
        kmax_sb = en(nc.sbuf_tensor("kmax_sb", [1, 16], F32))
        fr = en(nc.sbuf_tensor("fr", [1, 4 * QC], F32))
        ub64 = en(nc.sbuf_tensor("ub64", [128, 4 * QC], BF16))
        br = ub64
        avout = [en(nc.sbuf_tensor(f"avout{i}", [128, 2 * QC], F32)) for i in range(2)]
        xb = [en(nc.sbuf_tensor(f"xb{i}", [128, S], F32)) for i in range(2)]
        xn = en(nc.sbuf_tensor("xn", [128, NCH * (QC + 1)], BF16))
        xn1 = en(nc.sbuf_tensor("xn1", [128, S], BF16))
        xn1b = en(nc.sbuf_tensor("xn1b", [128, 4 * QC], BF16))
        b1 = en(nc.sbuf_tensor("b1", [128, S], BF16))
        b2_ = [en(nc.sbuf_tensor(f"b2_{i}", [128, S], BF16)) for i in range(2)]
        t0 = en(nc.sbuf_tensor("t0", [128, QC], F32))
        tb_ = [en(nc.sbuf_tensor(f"tb{i}", [128, QC], F32)) for i in range(2)]
        cast = [en(nc.sbuf_tensor(f"cast{i}", [128, 2 * QC], BF16)) for i in range(2)]
        stgk = [en(nc.sbuf_tensor(f"stgk{i}", [128, QC], BF16)) for i in range(2)]
        stgv = [en(nc.sbuf_tensor(f"stgv{i}", [128, 520], BF16)) for i in range(2)]
        kph = [en(nc.sbuf_tensor(f"kph{i}", [128, 16 * 128], BF16)) for i in range(3)]
        vph = [en(nc.sbuf_tensor(f"vph{i}", [128, 16 * 130], BF16)) for i in range(3)]
        ones_f32 = en(nc.sbuf_tensor("ones_f32", [128, 1], F32))
        pt = [en(nc.sbuf_tensor(f"pt{i}", [128, 1024], BF16)) for i in range(2)]
        q_sb = en(nc.sbuf_tensor("q_sb", [128, QC], BF16))

        pbig = en(nc.psum_tensor("pbig", [128, 2048], F32))
        pav = en(nc.psum_tensor("pav", [128, 1024], F32))
        pgen = en(nc.psum_tensor("pgen", [128, 1024], F32))

        P = B()
        T = Sem(en(nc.semaphore("T")))
        SS = Sem(en(nc.semaphore("SS")))
        V = Sem(en(nc.semaphore("V")))
        G = Sem(en(nc.semaphore("G")))
        d_x = [Sem(en(nc.semaphore(f"d_x{i}"))) for i in range(2)]
        d_sp = [Sem(en(nc.semaphore(f"d_sp{i}"))) for i in range(2)]
        d_k = [Sem(en(nc.semaphore(f"d_k{i}"))) for i in range(2)]
        d_v = [Sem(en(nc.semaphore(f"d_v{i}"))) for i in range(2)]
        d_gk = [Sem(en(nc.semaphore(f"d_gk{i}"))) for i in range(2)]
        d_gv = [Sem(en(nc.semaphore(f"d_gv{i}"))) for i in range(2)]
        d_w = Sem(en(nc.semaphore("d_w")))
        d_o = Sem(en(nc.semaphore("d_o")))
        d_n = Sem(en(nc.semaphore("d_n")))
        block = en(nc.Block())

        t_op = lambda fn: P.op("tensor", T, fn)
        s_op = lambda fn: P.op("scalar", SS, fn)
        v_op = lambda fn: P.op("vector", V, fn)
        g_op = lambda fn: P.op("gpsimd", G, fn)

        def hview(t_hbm, qc):
            return t_hbm[:, qc * QC:(qc + 1) * QC].rearrange("(r p) n -> p r n", p=128)

        def xt_like_xn1(qc):
            return xn1s[:, qc * QC:(qc + 1) * QC].rearrange("(r p) n -> p r n", p=128)

        def xv(i):
            return xb[i][:].rearrange("p (r n) -> p r n", r=NCH)

        def wslice(sb, kc, m):
            return sb[:, kc * C + m * 128: kc * C + (m + 1) * 128]

        def xn_c(r):
            return xn[:, r * (QC + 1) + 1: r * (QC + 1) + 1 + QC]

        def xn_p(r):
            return xn[:, r * (QC + 1): r * (QC + 1) + QC]

        r_mean = fr[0:1, 0 * QC:1 * QC]
        r_var = fr[0:1, 1 * QC:2 * QC]
        r_rstd = fr[0:1, 2 * QC:3 * QC]
        r_t = fr[0:1, 3 * QC:4 * QC]
        r_nmr = r_mean
        b_rhi = br[0:1, 0 * QC:1 * QC]
        b_rlo = br[0:1, 1 * QC:2 * QC]
        b_nhi = br[0:1, 2 * QC:3 * QC]
        b_nlo = br[0:1, 3 * QC:4 * QC]
        qbufs = (q_sb, stgk[0])

        P.dma("sync", d_x[0], lambda e: e.dma_start(out=xv(0), in_=hview(xt, 0)))
        P.dma("sync", d_x[1], lambda e: e.dma_start(out=xv(1), in_=hview(xt, 1)))
        P.dma("sync", d_w, lambda e: e.dma_start(out=bout_sb[:], in_=bout))
        P.dma("sync", d_w, lambda e: e.dma_start(out=mur_sb[:], in_=murh))
        P.dma("sync", d_w, lambda e: e.dma_start(out=muk_sb[:], in_=mukh))

        def wv_(sb):
            return sb[:].rearrange("p (kc d) -> p kc d", kc=NCH)

        def wh_(t_hbm):
            return t_hbm.rearrange("(kc p) d -> p kc d", p=128)

        for sb_t, hb in ((wr_sb, wrt), (wk_sb, wkt), (wv_sb, wvt), (wout_sb, woutt)):
            P.dma("gpsimd", d_w, lambda e, sb_t=sb_t, hb=hb: e.dma_start(
                out=wv_(sb_t), in_=wh_(hb)))
        for sb_t, hb in ((wqa_sb, wqa), (wka_sb, wka), (wva_sb, wva)):
            P.dma("gpsimd", d_w, lambda e, sb_t=sb_t, hb=hb: e.dma_start(
                out=sb_t[:], in_=hb))
        w_tick = d_w.n

        g_op(lambda e: e.memset(ones_sb[:], 1.0))
        g_op(lambda e: e.memset(ones_f32[:], 1.0))
        g_op(lambda e: e.memset(bnd1_sb[:], 0.0))
        g_op(lambda e: e.memset(bnd2_sb[:], 0.0))
        g_op(lambda e: e.memset(qmax_sb[:], 0.0))
        g_op(lambda e: e.memset(kmax_sb[:], 0.0))
        g_op(lambda e: e.memset(stgv[0][:], 1.0))
        init_tick = g_op(lambda e: e.memset(stgv[1][:], 1.0))

        P.wait("vector", G, init_tick)
        P.wait("tensor", G, init_tick)
        P.wait("tensor", d_w, w_tick)
        P.wait("scalar", G, init_tick)

        def newton_rsqrt(vrow, yrow, trow):
            irow = trow.bitcast(I32)
            v_op(lambda e: e.tensor_single_scalar(irow, vrow.bitcast(I32), 1,
                                                  ALU.logical_shift_right))
            v_op(lambda e: e.tensor_scalar(irow, irow, -1, MAGIC_RSQRT, ALU.mult, ALU.add))
            v_op(lambda e: e.tensor_copy(yrow, irow.bitcast(F32)))
            for _ in range(2):
                v_op(lambda e: e.tensor_tensor(trow, yrow, yrow, ALU.mult))
                v_op(lambda e: e.tensor_tensor(trow, trow, vrow, ALU.mult))
                v_op(lambda e: e.tensor_scalar(trow, trow, -0.5, 1.5, ALU.mult, ALU.add))
                v_op(lambda e: e.tensor_tensor(yrow, yrow, trow, ALU.mult))

        def newton_recip(vrow, yrow, trow):
            irow = trow.bitcast(I32)
            v_op(lambda e: e.tensor_scalar(irow, vrow.bitcast(I32), -1, MAGIC_RECIP,
                                           ALU.mult, ALU.add))
            v_op(lambda e: e.tensor_copy(yrow, irow.bitcast(F32)))
            for _ in range(2):
                v_op(lambda e: e.tensor_tensor(trow, vrow, yrow, ALU.mult))
                v_op(lambda e: e.tensor_scalar(trow, trow, -1.0, 2.0, ALU.mult, ALU.add))
                v_op(lambda e: e.tensor_tensor(yrow, yrow, trow, ALU.mult))

        def ln_stat_chunk(sv, r, deps=()):
            cb = cast[r % 2]
            for (sem, tick) in deps:
                P.wait("scalar", sem, tick)
            P.wf("scalar", f"cast{r % 2}")
            if r == 0:
                P.wf("tensor", "pg6")
                P.wf("tensor", "pg7")
            s_op(lambda e, r=r, cb=cb: e.activation(
                cb[:, 0:QC], sv[:, r, :], AF.Copy, bias=0.0, scale=1.0))
            st = s_op(lambda e, r=r, cb=cb: e.activation(
                cb[:, QC:2 * QC], sv[:, r, :], AF.Square, bias=0.0, scale=1.0))
            P.wait("tensor", SS, st)
            t_op(lambda e, cb=cb, r=r: e.matmul(
                pgen[0:1, 0:QC], ones_sb[:, 0:1], cb[:, 0:QC],
                start=(r == 0), stop=(r == NCH - 1)))
            m2 = t_op(lambda e, cb=cb, r=r: e.matmul(
                pgen[0:1, QC:2 * QC], ones_sb[:, 0:1], cb[:, QC:2 * QC],
                start=(r == 0), stop=(r == NCH - 1)))
            P.sf(f"cast{r % 2}", T, m2)
            return m2

        def ln_rows(last_mm):
            P.wait("vector", T, last_mm)
            v_op(lambda e: e.tensor_single_scalar(r_mean, pgen[0:1, 0:QC], 1.0 / C, ALU.mult))
            v_op(lambda e: e.tensor_single_scalar(r_var, pgen[0:1, QC:2 * QC], 1.0 / C, ALU.mult))
            v_op(lambda e: e.tensor_tensor(r_t, r_mean, r_mean, ALU.mult))
            v_op(lambda e: e.tensor_tensor(r_var, r_var, r_t, ALU.subtract))
            v_op(lambda e: e.tensor_scalar_add(r_var, r_var, EPS))
            newton_rsqrt(r_var, r_rstd, r_t)
            v_op(lambda e: e.scalar_tensor_tensor(r_nmr, r_mean, -1.0, r_rstd,
                                                  ALU.mult, ALU.mult))
            v_op(lambda e: e.tensor_copy(b_rhi, r_rstd))
            v_op(lambda e: e.tensor_tensor(b_rlo, r_rstd, b_rhi, ALU.subtract))
            v_op(lambda e: e.tensor_copy(b_nhi, r_nmr))
            return v_op(lambda e: e.tensor_tensor(b_nlo, r_nmr, b_nhi, ALU.subtract))

        def ln_reps(hl, target):
            if target == 'pgen':
                rsl, nsl = pgen[:, 0:QC], pgen[:, QC:2 * QC]
                k1, k2 = "pg6", "pg7"
            else:
                rsl, nsl = pav[:, 0:QC], pav[:, QC:2 * QC]
                k1, k2 = "pavH0", "pavH1"
            P.wf("tensor", k1)
            P.wf("tensor", k2)
            P.wait("tensor", V, hl)
            t_op(lambda e: e.matmul(rsl, ones_sb[0:1, :], b_rhi, start=True, stop=False))
            t_op(lambda e: e.matmul(rsl, ones_sb[0:1, :], b_rlo, start=False, stop=True))
            t_op(lambda e: e.matmul(nsl, ones_sb[0:1, :], b_nhi, start=True, stop=False))
            rep = t_op(lambda e: e.matmul(nsl, ones_sb[0:1, :], b_nlo, start=False, stop=True))
            return rep, rsl, nsl, k1, k2

        def ln_norm(sv, dst_fn, rep, rsl, nsl, k1, k2, pre=()):
            P.wait("vector", T, rep)
            for (sem, tick) in pre:
                P.wait("vector", sem, tick)
            last = 0
            for r in range(NCH):
                v_op(lambda e, r=r: e.tensor_tensor(
                    t0[:], sv[:, r, :], rsl, ALU.mult))
                last = v_op(lambda e, r=r: e.tensor_tensor(
                    dst_fn(r), t0[:], nsl, ALU.add))
            P.sf(k1, V, last)
            P.sf(k2, V, last)
            return last

        def ln(src, dst_fn, deps, target='pgen', pre=()):
            sv = src[:].rearrange("p (r n) -> p r n", r=NCH)
            last_mm = 0
            for r in range(NCH):
                last_mm = ln_stat_chunk(sv, r, deps if r == 0 else ())
            hl = ln_rows(last_mm)
            if target != 'pgen':
                P.sf("pg6", V, hl)
                P.sf("pg7", V, hl)
            rep, rsl, nsl, k1, k2 = ln_reps(hl, target)
            return ln_norm(sv, dst_fn, rep, rsl, nsl, k1, k2, pre)

        def boundary_restore(bnd):
            last = 0
            for r in range(NCH):
                last = v_op(lambda e, r=r, bnd=bnd: e.tensor_copy(
                    xn[:, r * (QC + 1): r * (QC + 1) + 1], bnd[:, r:r + 1]))
            return last

        def ffn_gen(src, bnd, b2buf, b2reg, banks, res):
            nb = len(banks)
            sv = src[:].rearrange("p (r n) -> p r n", r=NCH)
            b1v = b1[:].rearrange("p (r n) -> p r n", r=NCH)
            b2v = b2buf[:].rearrange("p (r n) -> p r n", r=NCH)
            xr_t = 0
            for r in range(NCH):
                if r == 0:
                    P.wf("vector", "b1")
                    P.wf("gpsimd", b2reg)
                dt_ = v_op(lambda e, r=r: e.tensor_tensor(b1v[:, r, :], xn_c(r), xn_p(r),
                                                          ALU.subtract))
                if r == 0:
                    P.wait("gpsimd", V, dt_)
                g_op(lambda e, r=r, b2v=b2v: e.tensor_tensor(b2v[:, r, :], xn_c(r), xn_p(r),
                                                    ALU.subtract))
                g_op(lambda e, r=r, b2v=b2v: e.tensor_tensor(
                    b2v[:, r, :], b2v[:, r, :],
                    mur_sb[:, r:r + 1].broadcast_to([128, QC]), ALU.mult))
                xr_t = g_op(lambda e, r=r, b2v=b2v: e.tensor_tensor(
                    b2v[:, r, :], b2v[:, r, :], xn_p(r), ALU.add))
                lerp_t = v_op(lambda e, r=r: e.scalar_tensor_tensor(
                    b1v[:, r, :], b1v[:, r, :], muk_sb[:, r:r + 1], xn_p(r),
                    ALU.mult, ALU.add))
                if r % 2 == 1:
                    yield "c"
            for r in range(NCH):
                v_op(lambda e, r=r, bnd=bnd: e.tensor_copy(
                    bnd[:, r:r + 1], xn[:, r * (QC + 1) + QC: r * (QC + 1) + QC + 1]))
            yield "c"
            P.wait("tensor", V, lerp_t)
            kact_first = True
            kact_t = 0
            for m in range(NCH):
                pb, pbreg = banks[m % nb]
                P.wf("tensor", pbreg)
                mt = 0
                for kc in range(NCH):
                    mt = t_op(lambda e, m=m, kc=kc, pb=pb: e.matmul(
                        pb, wslice(wk_sb, kc, m), b1v[:, kc, :],
                        start=(kc == 0), stop=(kc == NCH - 1)))
                    if FINE_YIELDS and kc < NCH - 1:
                        yield "f"
                tb = tb_[m % 2]
                P.wait("scalar", T, mt)
                P.wf("scalar", f"tb{m % 2}")
                rl = s_op(lambda e, pb=pb, tb=tb: e.activation(
                    tb[:], pb, AF.Relu, bias=0.0, scale=1.0))
                P.sf(pbreg, SS, rl)
                P.wait("vector", SS, rl)
                if kact_first:
                    P.wait("vector", G, xr_t)
                    kact_first = False
                kact_t = v_op(lambda e, m=m, tb=tb: e.tensor_tensor(
                    xn_c(m), tb[:], tb[:], ALU.mult))
                P.sf(f"tb{m % 2}", V, kact_t)
                yield "c"
            out_t = 0
            out_ts = []
            for m in range(NCH):
                pr, prreg = banks[m % 2]
                pv, pvreg = banks[2 + (m + 1) % 2 if nb == 4 else (m + 1) % 2]
                P.wf("tensor", prreg)
                if m == 0:
                    P.wait("tensor", G, xr_t)
                mt = 0
                for kc in range(NCH):
                    mt = t_op(lambda e, m=m, kc=kc, pr=pr, b2v=b2v: e.matmul(
                        pr, wslice(wr_sb, kc, m), b2v[:, kc, :],
                        start=(kc == 0), stop=(kc == NCH - 1)))
                    if FINE_YIELDS and kc < NCH - 1:
                        yield "f"
                tb = tb_[m % 2]
                P.wait("scalar", T, mt)
                P.wf("scalar", f"tb{m % 2}")
                th = s_op(lambda e, pr=pr, tb=tb: e.activation(
                    tb[:], pr, AF.Tanh, bias=0.0, scale=0.5))
                P.sf(prreg, SS, th)
                P.wf("tensor", pvreg)
                if m == 0:
                    P.wait("tensor", V, kact_t)
                if FINE_YIELDS:
                    yield "f"
                mtv = 0
                for kc in range(NCH):
                    mtv = t_op(lambda e, m=m, kc=kc, pv=pv: e.matmul(
                        pv, wslice(wv_sb, kc, m), xn_c(kc),
                        start=(kc == 0), stop=(kc == NCH - 1)))
                    if FINE_YIELDS and kc < NCH - 1:
                        yield "f"
                P.wait("vector", SS, th)
                P.wait("vector", T, mtv)
                ct = v_op(lambda e, tb=tb, pv=pv: e.scalar_tensor_tensor(
                    t0[:], tb[:], 1.0, pv, ALU.add, ALU.mult))
                P.sf(f"tb{m % 2}", V, ct)
                P.sf(pvreg, V, ct)
                out_t = v_op(lambda e, m=m, sv=sv: e.scalar_tensor_tensor(
                    sv[:, m, :], t0[:], 0.25, sv[:, m, :], ALU.mult, ALU.add))
                out_ts.append(out_t)
                yield "c"
            P.sf("b1", V, out_t)
            P.sf(b2reg, V, out_t)
            P.sf("xnreg", T, mtv)
            res['out_t'] = out_t
            res['out_ts'] = out_ts

        PB_BANKS = [(pbig[:, i * QC:(i + 1) * QC], f"pb{i}") for i in range(4)]
        PG_BANKS = [(pgen[:, 0:QC], "pg6"), (pgen[:, QC:2 * QC], "pg7")]
        spill_tick = [0, 0]
        a_n2t = [0, 0]
        a_spill_xn1 = []

        def ln1_emit(qc_):
            b_ = qc_ % 2
            pre = [P.free["xnreg"]] if "xnreg" in P.free else []
            return ln(xb[b_], xn_c, [(d_x[b_], 16 * (qc_ // 2 + 1))],
                      target='pav', pre=pre)

        ln1_emit(0)
        for qc in range(nqc):
            buf = qc % 2
            src = xb[buf]
            boundary_restore(bnd1_sb)
            fres = {}
            for _ in ffn_gen(src, bnd1_sb, b2_[buf], f"b2{buf}", PB_BANKS, fres):
                pass
            ffn_t, ffn_ts = fres['out_t'], fres['out_ts']
            P.wait("scalar", V, ffn_t)
            sp = P.dma("scalar", d_sp[buf], lambda e, qc=qc, src=src: e.dma_start(
                out=hview(x1s, qc), in_=src[:].rearrange("p (r n) -> p r n", r=NCH)))
            spill_tick[buf] = sp
            if stage == 'ffn':
                continue
            xn1v = xn1[:].rearrange("p (r n) -> p r n", r=NCH)
            sv2 = src[:].rearrange("p (r n) -> p r n", r=NCH)
            lm2 = 0
            for m in range(NCH):
                lm2 = ln_stat_chunk(sv2, m, deps=[(V, ffn_ts[m])])
            hl2_ = ln_rows(lm2)
            rep2, rsl2, nsl2, k12, k22 = ln_reps(hl2_, 'pgen')
            P.wf("vector", "xn1w")
            P.wf("vector", "xn1r")
            n2t = ln_norm(sv2, lambda r: xn1v[:, r, :], rep2, rsl2, nsl2, k12, k22)
            a_n2t[buf] = n2t
            if qc + 1 < nqc:
                ln1_emit(qc + 1)
            P.wait("scalar", V, n2t)
            xn1_spill = P.dma("scalar", d_n, lambda e, qc=qc: e.dma_start(
                out=xt_like_xn1(qc),
                in_=xn1[:].rearrange("p (r n) -> p r n", r=NCH)))
            P.sf("xn1w", d_n, xn1_spill)
            a_spill_xn1.append(xn1_spill)
            last_q_mm = 0

            def kv_tail(c):
                p_, kb, qb, kmt_, qmt_ = c
                pairch = xn1v[:, p_, :]
                sg = p_ % 2
                kslice = pav[:, kb * QC:(kb + 1) * QC]
                qslice = pbig[:, qb * QC:(qb + 1) * QC]
                P.wait("scalar", T, kmt_)
                P.wait("scalar", d_gk[sg], d_gk[sg].n)
                ke = s_op(lambda e, sg=sg, kslice=kslice: e.activation(
                    stgk[sg][:], kslice, AF.Copy, bias=0.0, scale=1.0))
                P.wait("scalar", SS, ke)
                P.dma("scalar", d_gk[sg], lambda e, p_=p_, qc=qc, sg=sg: e.dma_start(
                    out=ks[p_, :, qc * QC:(qc + 1) * QC], in_=stgk[sg][:]))
                P.wf("scalar", "cast0")
                ksq = s_op(lambda e, kslice=kslice: e.activation(
                    cast[0][:, QC:2 * QC], kslice, AF.Square, bias=0.0, scale=1.0))
                P.sf(f"pavH{kb}", SS, ksq)
                P.wait("tensor", SS, ksq)
                P.wf("tensor", "pg6")
                ksm = t_op(lambda e: e.matmul(pgen[0:1, 0:QC], ones_sb[:, 0:1],
                                              cast[0][:, QC:2 * QC], start=True, stop=True))
                P.sf("cast0", T, ksm)
                P.wait("vector", T, ksm)
                krd = v_op(lambda e: e.reduce_max(r_t[0:1, 0:1], pgen[0:1, 0:QC], axis=AX))
                P.wait("vector", V, krd)
                kred = v_op(lambda e, p_=p_: e.tensor_tensor(
                    kmax_sb[0:1, p_:p_ + 1], kmax_sb[0:1, p_:p_ + 1], r_t[0:1, 0:1], ALU.max))
                P.sf("pg6", V, kred)
                P.wait("scalar", T, qmt_)
                P.wf("scalar", "cast1")
                qsq = s_op(lambda e, qslice=qslice: e.activation(
                    cast[1][:, QC:2 * QC], qslice, AF.Square, bias=0.0, scale=INV32))
                P.sf(f"pb{qb}", SS, qsq)
                P.wait("tensor", SS, qsq)
                P.wf("tensor", "pg7")
                qsm = t_op(lambda e: e.matmul(pgen[0:1, QC:2 * QC], ones_sb[:, 0:1],
                                              cast[1][:, QC:2 * QC], start=True, stop=True))
                P.sf("cast1", T, qsm)
                P.wait("vector", T, qsm)
                qrd = v_op(lambda e: e.reduce_max(r_t[0:1, 1:2], pgen[0:1, QC:2 * QC], axis=AX))
                P.wait("vector", V, qrd)
                qred = v_op(lambda e, p_=p_: e.tensor_tensor(
                    qmax_sb[0:1, p_:p_ + 1], qmax_sb[0:1, p_:p_ + 1], r_t[0:1, 1:2], ALU.max))
                P.sf("pg7", V, qred)
                P.wf("tensor", "pb0")
                P.wf("tensor", "pb1")
                vmt_ = 0
                for tc in range(4):
                    lo = tc * 128
                    P.raw("tensor", lambda e, pairch=pairch, lo=lo, tc=tc: e.matmul(
                        pbig[:, tc * 64: tc * 64 + 64], pairch[0:64, lo:lo + 128],
                        wva_sb[0:64, :], start=True, stop=True, tile_position=(0, 0)))
                    vmt_ = t_op(lambda e, pairch=pairch, lo=lo, tc=tc: e.matmul(
                        pbig[:, QC + tc * 64: QC + tc * 64 + 64], pairch[64:128, lo:lo + 128],
                        wva_sb[64:128, :], start=True, stop=True, tile_position=(64, 0)))
                sgv = p_ % 2
                P.wait("vector", T, vmt_)
                P.wait("vector", d_gv[sgv], d_gv[sgv].n)
                stv = stgv[sgv][:].rearrange("q (t c) -> q t c", c=130)
                pa = pbig[:, 0:256].rearrange("q (t c) -> q t c", c=64)
                pb_ = pbig[:, QC:QC + 256].rearrange("q (t c) -> q t c", c=64)
                v_op(lambda e, stv=stv, pa=pa: e.tensor_copy(stv[:, :, 0:64], pa))
                vev = v_op(lambda e, stv=stv, pb_=pb_: e.tensor_copy(stv[:, :, 65:129], pb_))
                P.sf("pb0", V, vev)
                P.sf("pb1", V, vev)
                P.wait("scalar", V, vev)
                P.dma("scalar", d_gv[sgv], lambda e, p_=p_, qc=qc, sgv=sgv: e.dma_start(
                    out=vs[p_, :, qc * 520:(qc + 1) * 520], in_=stgv[sgv][:]))
                return vmt_

            prevkv = None
            for p in range(NPAIR):
                pairch = xn1v[:, p, :]
                kb = p % 2
                qb = 2 + p % 2
                P.wf("tensor", f"pavH{kb}")
                if p == 0:
                    P.wait("tensor", V, n2t)
                P.raw("tensor", lambda e, pairch=pairch, kb=kb: e.matmul(
                    pav[0:64, kb * QC:kb * QC + QC], wka_sb[0:64, :], pairch[0:64, :],
                    start=True, stop=True, tile_position=(0, 0)))
                kmt = t_op(lambda e, pairch=pairch, kb=kb: e.matmul(
                    pav[64:128, kb * QC:kb * QC + QC], wka_sb[64:128, :], pairch[64:128, :],
                    start=True, stop=True, tile_position=(64, 64)))
                P.wf("tensor", f"pb{qb}")
                P.raw("tensor", lambda e, pairch=pairch, qb=qb: e.matmul(
                    pbig[0:64, qb * QC:qb * QC + QC], wqa_sb[0:64, :], pairch[0:64, :],
                    start=True, stop=True, tile_position=(0, 0)))
                qmt = t_op(lambda e, pairch=pairch, qb=qb: e.matmul(
                    pbig[64:128, qb * QC:qb * QC + QC], wqa_sb[64:128, :], pairch[64:128, :],
                    start=True, stop=True, tile_position=(64, 64)))
                last_q_mm = qmt
                if prevkv is not None:
                    kv_tail(prevkv)
                prevkv = (p, kb, qb, kmt, qmt)
            last_vmt = kv_tail(prevkv)
            last_q_mm = max(last_q_mm, last_vmt)
            P.sf("xn1r", T, last_q_mm)
            if qc + 2 < nqc:
                P.wait("sync", d_sp[buf], spill_tick[buf])
                P.wait("sync", V, n2t)
                P.dma("sync", d_x[buf], lambda e, qc=qc, buf=buf: e.dma_start(
                    out=xv(buf), in_=hview(xt, qc + 2)))

        t1_ = v_op(lambda e: e.tensor_tensor(r_t[0:1, 0:NPAIR], qmax_sb[0:1, 0:NPAIR],
                                             kmax_sb[0:1, 0:NPAIR], ALU.mult))
        P.wait("vector", V, t1_)
        t2_ = v_op(lambda e: e.tensor_scalar(r_t[0:1, 0:NPAIR], r_t[0:1, 0:NPAIR], 0.51, 1.0,
                                             ALU.mult, ALU.add))
        P.wait("vector", V, t2_)
        t3_ = v_op(lambda e: e.tensor_copy(br[0:1, 0:NPAIR], r_t[0:1, 0:NPAIR]))
        P.wait("vector", V, t3_)
        chl = v_op(lambda e: e.tensor_tensor(br[0:1, NPAIR:2 * NPAIR],
                                             r_t[0:1, 0:NPAIR], br[0:1, 0:NPAIR],
                                             ALU.subtract))
        P.wf("tensor", "pg6")
        P.wait("tensor", V, chl)
        t_op(lambda e: e.matmul(pgen[:, 0:NPAIR], ones_sb[0:1, :], br[0:1, 0:NPAIR],
                                start=True, stop=False))
        cm = t_op(lambda e: e.matmul(pgen[:, 0:NPAIR], ones_sb[0:1, :],
                                     br[0:1, NPAIR:2 * NPAIR], start=False, stop=True))
        P.wait("scalar", T, cm)
        negc_t = s_op(lambda e: e.activation(negc_sb[:], pgen[:, 0:NPAIR], AF.Copy,
                                             bias=0.0, scale=-1.0))
        P.sf("pg6", SS, negc_t)
        if debug:
            P.wait("scalar", SS, negc_t)
            P.dma("scalar", d_w, lambda e: e.dma_start(out=dbgn, in_=negc_sb[:]))
            P.dma("scalar", d_w, lambda e: e.dma_start(out=dbgs[0:1, :], in_=qmax_sb[:]))
            P.dma("scalar", d_w, lambda e: e.dma_start(out=dbgs[1:2, :], in_=kmax_sb[:]))

        if not do_b:
            P.wait("sync", d_sp[0], d_sp[0].n)
            P.wait("sync", d_sp[1], d_sp[1].n)
            for s_ in (d_gk[0], d_gk[1], d_gv[0], d_gv[1]):
                P.wait("sync", s_, s_.n)
            P.run(block)
            return nc
        P.wait("sync", d_sp[0], spill_tick[0])
        P.wait("sync", V, a_n2t[0])
        ld0 = P.dma("sync", d_x[0], lambda e: e.dma_start(out=xv(0), in_=hview(x1s, 0)))
        P.wait("sync", d_sp[1], spill_tick[1])
        P.wait("sync", V, a_n2t[1])
        ld1 = P.dma("sync", d_x[1], lambda e: e.dma_start(out=xv(1), in_=hview(x1s, 1)))
        x_load_tick = [ld0, ld1]
        for s_ in (d_gk[0], d_gk[1], d_gv[0], d_gv[1]):
            P.wait("sync", s_, s_.n)
        NH = 2 * nqc * NPAIR
        kht = [0] * NH
        vht = [0] * NH
        kfree = [0] * NH
        vfree = [0] * NH

        def kv_load(h):
            pr_ = (h // 2) % NPAIR
            hb = (h % 2)
            kht[h] = P.dma("sync", d_k[0], lambda e, h=h, pr_=pr_, hb=hb: e.dma_start(
                out=kph[h % 3][:], in_=ks[pr_, :, hb * 2048:(hb + 1) * 2048]))
            vht[h] = P.dma("sync", d_v[0], lambda e, h=h, pr_=pr_, hb=hb: e.dma_start(
                out=vph[h % 3][:], in_=vs[pr_, :, hb * 2080:(hb + 1) * 2080]))

        for h0 in range(3):
            kv_load(h0)
        P.wait("tensor", d_k[0], kht[2])
        P.wait("tensor", d_v[0], vht[2])
        pt_guard = [0, 0]

        def norm_rows(ab2):
            u_y = avout[1 - ab2][64:65, 0:QC]
            u_t = avout[1 - ab2][64:65, QC:2 * QC]
            hl2 = 0
            for hh in range(2):
                vrow = avout[ab2][64:65, hh * QC:(hh + 1) * QC]
                newton_recip(vrow, u_y, u_t)
                v_op(lambda e, hh=hh, u_y=u_y: e.tensor_copy(
                    ub64[64:65, (2 * hh) * QC:(2 * hh + 1) * QC], u_y))
                hl2 = v_op(lambda e, hh=hh, u_y=u_y: e.tensor_tensor(
                    ub64[64:65, (2 * hh + 1) * QC:(2 * hh + 2) * QC], u_y,
                    ub64[64:65, (2 * hh) * QC:(2 * hh + 1) * QC], ALU.subtract))
            return hl2

        odo_tick = [0, 0]

        def norm_apply(pp, ab2, hl2, b2vq, b2q):
            sgo = pp % 2
            P.wf("tensor", "pg7")
            P.wait("tensor", V, hl2)
            P.raw("tensor", lambda e: e.matmul(
                pgen[0:64, QC:2 * QC], ones_sb[64:65, 0:64],
                ub64[64:65, 0:QC], start=True, stop=False, tile_position=(64, 0)))
            P.raw("tensor", lambda e: e.matmul(
                pgen[0:64, QC:2 * QC], ones_sb[64:65, 0:64],
                ub64[64:65, QC:2 * QC], start=False, stop=True,
                tile_position=(64, 0)))
            P.wf("tensor", "pg6")
            P.raw("tensor", lambda e: e.matmul(
                pgen[0:64, 0:QC], ones_sb[64:65, 0:64],
                ub64[64:65, 2 * QC:3 * QC], start=True, stop=False,
                tile_position=(64, 0)))
            rpt = t_op(lambda e: e.matmul(
                pgen[0:64, 0:QC], ones_sb[64:65, 0:64],
                ub64[64:65, 3 * QC:4 * QC], start=False, stop=True,
                tile_position=(64, 0)))
            P.wait("vector", T, rpt)
            aA = v_op(lambda e, pp=pp, ab2=ab2, b2vq=b2vq: e.tensor_tensor(
                b2vq[0:64, pp, :], avout[ab2][0:64, 0:QC],
                pgen[0:64, QC:2 * QC], ALU.mult))
            P.sf("pg7", V, aA)
            P.wait("vector", d_o, odo_tick[sgo])
            last = v_op(lambda e, ab2=ab2, sgo=sgo: e.tensor_tensor(
                stgv[sgo][0:64, 0:QC], avout[ab2][0:64, QC:2 * QC],
                pgen[0:64, 0:QC], ALU.mult))
            P.sf("pg6", V, last)
            P.sf(f"avout{ab2}", V, last)
            P.wait("gpsimd", V, last)
            odo_tick[sgo] = P.dma("gpsimd", d_o, lambda e, pp=pp, sgo=sgo, b2q=b2q:
                e.dma_start(out=b2q[64:128, pp * QC:(pp + 1) * QC],
                            in_=stgv[sgo][0:64, 0:QC]))
            return last

        NSLOT = 3
        slot_tick = [0] * NSLOT
        slot_qmt = [0] * NSLOT

        def xn1b_slot(g):
            s = g % NSLOT
            return xn1b[:, s * QC:(s + 1) * QC]

        def ring_load(g):
            s = g % NSLOT
            qc_, p_ = g // NPAIR, g % NPAIR
            P.wait("sync", d_n, a_spill_xn1[qc_])
            P.wait("sync", T, slot_qmt[s])
            slot_tick[s] = P.dma("sync", d_n, lambda e, g=g, qc_=qc_, p_=p_:
                e.dma_start(out=xn1b_slot(g),
                            in_=xn1s[128 * p_:128 * (p_ + 1),
                                     qc_ * QC:(qc_ + 1) * QC]))

        for g0 in range(NSLOT - 1):
            ring_load(g0)
        P.wait("vector", d_gv[0], d_gv[0].n)
        P.wait("vector", d_gv[1], d_gv[1].n)
        P.wait("vector", d_gk[1], d_gk[1].n)
        P.sf("qb1", d_gk[0], d_gk[0].n)
        def epilogue(eqc, enorm_t, eod):
            ebuf = eqc % 2
            esrc = xb[ebuf]
            eb2v = b2_[ebuf][:].rearrange("p (r n) -> p r n", r=NCH)
            xsv = esrc[:].rearrange("p (r n) -> p r n", r=NCH)
            s1 = t0[:]
            s2 = cast[0][:].bitcast(F32)[:, 0:QC]
            sq = cast[1][:].bitcast(F32)[:, 0:QC]
            P.wait("tensor", V, enorm_t)
            P.wait("tensor", d_o, eod)
            P.wait("tensor", d_x[ebuf], x_load_tick[ebuf])
            s_t = 0
            for m in range(NCH):
                pb, pbreg = PG_BANKS[m % 2]
                P.wf("tensor", pbreg)
                mt = 0
                for kc in range(NCH):
                    mt = t_op(lambda e, m=m, kc=kc, pb=pb, eb2v=eb2v: e.matmul(
                        pb, wslice(wout_sb, kc, m), eb2v[:, kc, :],
                        start=(kc == 0), stop=(kc == NCH - 1)))
                    if FINE_YIELDS and kc < NCH - 1:
                        yield "f"
                tb = tb_[m % 2]
                P.wait("vector", T, mt)
                P.wf("vector", f"tb{m % 2}")
                v_op(lambda e, pb=pb, tb=tb, m=m: e.tensor_tensor(
                    tb[:], pb, bout_sb[:, m:m + 1].broadcast_to([128, QC]), ALU.add))
                proj_t = v_op(lambda e, m=m, tb=tb, xsv=xsv: e.tensor_tensor(
                    xsv[:, m, :], xsv[:, m, :], tb[:], ALU.add))
                P.sf(pbreg, V, proj_t)
                P.sf(f"tb{m % 2}", V, proj_t)
                if m == 0:
                    v_op(lambda e, s1=s1, xsv=xsv: e.tensor_copy(s1, xsv[:, 0, :]))
                    s_t = v_op(lambda e, s2=s2, xsv=xsv: e.tensor_tensor(
                        s2, xsv[:, 0, :], xsv[:, 0, :], ALU.mult))
                else:
                    v_op(lambda e, m=m, s1=s1, xsv=xsv: e.tensor_tensor(
                        s1, s1, xsv[:, m, :], ALU.add))
                    v_op(lambda e, m=m, sq=sq, xsv=xsv: e.tensor_tensor(
                        sq, xsv[:, m, :], xsv[:, m, :], ALU.mult))
                    s_t = v_op(lambda e, s2=s2, sq=sq: e.tensor_tensor(
                        s2, s2, sq, ALU.add))
                if m == NCH - 1:
                    P.sf(f"b2{ebuf}", T, mt)
                yield "c"
            if stage == 'bx2':
                if eqc == 0:
                    P.wait("scalar", V, enorm_t)
                    P.wait("scalar", d_o, eod)
                    P.dma("scalar", d_w, lambda e: e.dma_start(
                        out=dbga, in_=b2_[0][:]))
                P.wait("scalar", V, s_t)
                yst = P.dma("scalar", d_sp[ebuf], lambda e, eqc=eqc, esrc=esrc:
                    e.dma_start(out=hview(yt, eqc),
                                in_=esrc[:].rearrange("p (r n) -> p r n", r=NCH)))
                if eqc + 2 < nqc:
                    P.wait("sync", d_sp[ebuf], yst)
                    x_load_tick[ebuf] = P.dma("sync", d_x[ebuf],
                        lambda e, eqc=eqc, ebuf=ebuf: e.dma_start(
                            out=xv(ebuf), in_=hview(x1s, eqc + 2)))
                return
            yield "b"
            s1b = cast[1][:, 0:QC]
            s2hi = cast[1][:, QC:2 * QC]
            s2lo = stgk[1][:, 0:QC]
            v_op(lambda e, s1=s1, s1b=s1b: e.tensor_copy(s1b, s1))
            v_op(lambda e, s2=s2, s2hi=s2hi: e.tensor_copy(s2hi, s2))
            s_t = v_op(lambda e, s2=s2, s2hi=s2hi, s2lo=s2lo: e.tensor_tensor(
                s2lo, s2, s2hi, ALU.subtract))
            P.wf("tensor", "pg6")
            P.wf("tensor", "pg7")
            P.wait("tensor", V, s_t)
            t_op(lambda e, s1b=s1b: e.matmul(pgen[0:1, 0:QC], ones_sb[:, 0:1], s1b,
                                             start=True, stop=True))
            P.raw("tensor", lambda e, s2hi=s2hi: e.matmul(
                pgen[0:1, QC:2 * QC], ones_sb[:, 0:1], s2hi, start=True, stop=False))
            lm3 = t_op(lambda e, s2lo=s2lo: e.matmul(
                pgen[0:1, QC:2 * QC], ones_sb[:, 0:1], s2lo, start=False, stop=True))
            hl3 = ln_rows(lm3)
            P.sf("pg6", V, hl3)
            P.sf("pg7", V, hl3)
            for _ in range(22):
                yield "c"
            rep3, rsl3, nsl3, k13, k23 = ln_reps(hl3, 'pgen')
            sv3 = xsv
            last_n = 0
            P.wait("vector", T, rep3)
            for r in range(NCH):
                v_op(lambda e, r=r, sv3=sv3: e.tensor_tensor(
                    t0[:], sv3[:, r, :], rsl3, ALU.mult))
                last_n = v_op(lambda e, r=r: e.tensor_tensor(
                    xn_c(r), t0[:], nsl3, ALU.add))
            P.sf(k13, V, last_n)
            P.sf(k23, V, last_n)
            boundary_restore(bnd2_sb)
            yield "c"
            fres2 = {}
            yield from ffn_gen(esrc, bnd2_sb, b2_[ebuf], f"b2{ebuf}", PG_BANKS, fres2)
            y_t = fres2['out_t']
            P.wait("scalar", V, y_t)
            yst = P.dma("scalar", d_sp[ebuf], lambda e, eqc=eqc, esrc=esrc:
                e.dma_start(out=hview(yt, eqc),
                            in_=esrc[:].rearrange("p (r n) -> p r n", r=NCH)))
            if eqc + 2 < nqc:
                P.wait("sync", d_sp[ebuf], yst)
                x_load_tick[ebuf] = P.dma("sync", d_x[ebuf],
                    lambda e, eqc=eqc, ebuf=ebuf: e.dma_start(
                        out=xv(ebuf), in_=hview(x1s, eqc + 2)))

        ep_state = {"gen": None, "parked": False, "mid": False}

        def _ep_next():
            f = next(ep_state["gen"], None)
            if f is None:
                ep_state["gen"] = None
                ep_state["mid"] = False
            elif f == "b":
                ep_state["parked"] = True
                ep_state["mid"] = False
            else:
                ep_state["mid"] = (f == "f")

        def ep_step(k, unpark=False):
            if ep_state["parked"]:
                if not unpark:
                    return
                ep_state["parked"] = False
            for _ in range(k):
                if ep_state["gen"] is None or ep_state["parked"]:
                    return
                _ep_next()

        def ep_drain_coarse(cap=16):
            while (ep_state["mid"] and cap > 0 and ep_state["gen"] is not None
                   and not ep_state["parked"]):
                _ep_next()
                cap -= 1

        for qc in range(nqc):
            buf = qc % 2
            src = xb[buf]
            b2q = b2_[buf]
            b2vq = b2q[:].rearrange("p (r n) -> p r n", r=NCH)
            prev_pair = None
            last_q_mm = 0
            norm_t = 0

            def qproj(p_):
                P.wf("tensor", "pg7")
                g_ = qc * NPAIR + p_
                P.wait("tensor", d_n, slot_tick[g_ % NSLOT])
                pairch = xn1b_slot(g_)
                P.raw("tensor", lambda e, pairch=pairch: e.matmul(
                    pgen[0:64, QC:2 * QC], wqa_sb[0:64, :], pairch[0:64, :],
                    start=True, stop=True, tile_position=(0, 0)))
                qmt_ = t_op(lambda e, pairch=pairch: e.matmul(
                    pgen[64:128, QC:2 * QC], wqa_sb[64:128, :], pairch[64:128, :],
                    start=True, stop=True, tile_position=(64, 64)))
                qb_ = qbufs[p_ % 2]
                P.wait("vector", T, qmt_)
                P.wf("vector", f"qb{p_ % 2}")
                qe_ = v_op(lambda e, qb_=qb_: e.tensor_single_scalar(
                    qb_[:, 0:QC], pgen[:, QC:2 * QC], INV32, ALU.mult))
                P.sf("pg7", V, qe_)
                slot_qmt[g_ % NSLOT] = qmt_
                if g_ + NSLOT - 1 < nqc * NPAIR:
                    ring_load(g_ + NSLOT - 1)
                return qmt_, qe_

            q_next = qproj(0)
            for p in range(NPAIR):
                gidx = qc * NPAIR + p
                h0, h1 = 2 * gidx, 2 * gidx + 1
                qmt, qe = q_next
                last_q_mm = qmt
                q_sbv = qbufs[p % 2]
                ep_drain_coarse()
                if p + 1 < NPAIR:
                    q_next = qproj(p + 1)
                ep_step(1, unpark=True)
                exp_ticks = [0] * KCN
                exp_av = [0] * KCN

                def emit_av(kc):
                    sb2 = kc % 2
                    h = h0 if kc < 16 else h1
                    if kc == 0:
                        P.wf("tensor", "pavB")
                    if kc % 16 == 0:
                        P.wait("tensor", d_v[0], vht[h])
                    P.wait("tensor", SS, exp_ticks[kc])
                    vb = vph[h % 3]
                    c0 = (kc % 16) * 130
                    P.raw("tensor", lambda e, sb2=sb2, vb=vb, c0=c0: e.matmul(
                        pav[0:65, 0:QC], vb[:, c0: c0 + 65],
                        pt[sb2][:, 0:QC], start=(kc == 0), stop=(kc == KCN - 1)))
                    t_ = t_op(lambda e, sb2=sb2, vb=vb, c0=c0: e.matmul(
                        pav[0:65, QC:2 * QC], vb[:, c0 + 65: c0 + 130],
                        pt[sb2][:, QC:2 * QC], start=(kc == 0), stop=(kc == KCN - 1)))
                    exp_av[kc] = t_
                    vfree[h] = t_
                    return t_

                def emit_sc(kc):
                    sb_ = kc % 2
                    h = h0 if kc < 16 else h1
                    if kc < 2:
                        P.wf("tensor", f"pbB{sb_}")
                        P.wf("tensor", f"pb{sb_ * 2}")
                        P.wf("tensor", f"pb{sb_ * 2 + 1}")
                        if kc == 0:
                            P.wait("tensor", V, qe)
                    else:
                        P.wait("tensor", SS, exp_ticks[kc - 2])
                    if kc % 16 == 0:
                        P.wait("tensor", d_k[0], kht[h])
                    kb = kph[h % 3]
                    c0 = (kc % 16) * 128
                    P.raw("tensor", lambda e, sb_=sb_, kb=kb, c0=c0, q_sbv=q_sbv:
                        e.matmul(
                            pbig[:, sb_ * 1024: sb_ * 1024 + QC],
                            kb[0:64, c0:c0 + 128],
                            q_sbv[0:64, 0:QC], start=True, stop=True,
                            tile_position=(0, 0)))
                    mm = t_op(lambda e, sb_=sb_, kb=kb, c0=c0, q_sbv=q_sbv:
                        e.matmul(
                            pbig[:, sb_ * 1024 + QC: sb_ * 1024 + 2 * QC],
                            kb[64:128, c0:c0 + 128],
                            q_sbv[64:128, 0:QC], start=True, stop=True,
                            tile_position=(64, 0)))
                    kfree[h] = mm
                    return mm

                def emit_exp(kc, sct):
                    sb_ = kc % 2
                    P.wait("scalar", T, sct)
                    if kc >= 2:
                        P.wait("scalar", T, exp_av[kc - 2])
                    else:
                        P.wait("scalar", T, pt_guard[sb_])
                    exp_ticks[kc] = s_op(lambda e, sb_=sb_, p=p: e.activation(
                        pt[sb_][:], pbig[:, sb_ * 1024: sb_ * 1024 + 2 * QC],
                        AF.Exp, bias=negc_sb[:, p:p + 1], scale=1.0))
                    if debug and qc == 0 and p == 0:
                        P.wait("scalar", SS, exp_ticks[kc])
                        P.dma("scalar", d_w, lambda e, sb_=sb_, kc=kc: e.dma_start(
                            out=dbgp[:, kc * 1024:(kc + 1) * 1024], in_=pt[sb_][:]))
                        P.wait("scalar", d_w, d_w.n)

                for kc in range(KCN):
                    s0 = emit_sc(kc)
                    if kc >= 1:
                        emit_av(kc - 1)
                    emit_exp(kc, s0)
                    ep_step(1 + kc % 2)
                    if kc == 17 and h0 + 3 < NH:
                        P.wait("sync", T, T.n if STRICT_KV else
                               max(kfree[h0], vfree[h0]))
                        kv_load(h0 + 3)
                av_last = emit_av(KCN - 1)
                P.sf(f"pbB0", SS, exp_ticks[KCN - 2])
                P.sf(f"pbB1", SS, exp_ticks[KCN - 1])
                pt_guard[0] = exp_av[KCN - 2]
                pt_guard[1] = exp_av[KCN - 1]
                P.sf(f"qb{p % 2}", T, av_last)
                if h1 + 3 < NH:
                    P.wait("sync", T, T.n if STRICT_KV else
                           max(kfree[h1], vfree[h1]))
                    kv_load(h1 + 3)
                ab = p % 2
                P.wait("vector", T, av_last)
                P.wf("vector", f"avout{ab}")
                ave = v_op(lambda e, ab=ab: e.tensor_copy(
                    avout[ab][0:65, 0:2 * QC], pav[0:65, 0:2 * QC]))
                P.sf("pavB", V, ave)
                if debug and qc == 0 and p == 0:
                    P.wait("scalar", V, ave)
                    P.dma("scalar", d_w, lambda e, ab=ab: e.dma_start(
                        out=dbga2, in_=avout[ab][:]))
                    P.wait("vector", d_w, d_w.n)
                ep_drain_coarse()
                if prev_pair is not None:
                    norm_t = norm_apply(*prev_pair, b2vq, b2q)
                prev_pair = (p, ab, norm_rows(ab))
            ep_drain_coarse()
            norm_t = norm_apply(*prev_pair, b2vq, b2q)
            while ep_state["gen"] is not None:
                ep_step(100000, unpark=True)
            ep_state["gen"] = epilogue(qc, norm_t, d_o.n)
            if NO_OVERLAP:
                while ep_state["gen"] is not None:
                    ep_step(100000, unpark=True)

        while ep_state["gen"] is not None:
            ep_step(100000, unpark=True)
        P.wait("sync", d_sp[0], d_sp[0].n)
        P.wait("sync", d_sp[1], d_sp[1].n)

        P.run(block)

    return nc


_NC_CACHE = {}


def _get_nc():
    if "nc" not in _NC_CACHE:
        _NC_CACHE["nc"] = build_nc()
    return _NC_CACHE["nc"]


def make_in_maps(L0_attn_x, L0_attn_Wq, L0_attn_Wk, L0_attn_Wv, L0_attn_W_out,
                 L0_attn_b_out, L0_ffn_W_r, L0_ffn_W_k, L0_ffn_W_v, L0_ffn_mu_r,
                 L0_ffn_mu_k):
    x = np.asarray(L0_attn_x, dtype=np.float32)
    f32c = lambda a: np.ascontiguousarray(np.asarray(a, dtype=np.float32))

    def dualT(w):
        wt = f32c(np.asarray(w, dtype=np.float32).T)
        return np.ascontiguousarray(np.concatenate([wt, wt], axis=0))

    def chv(v):
        return np.ascontiguousarray(np.asarray(v, dtype=np.float32).reshape(8, 128).T)

    base = {
        "wrt": f32c(np.asarray(L0_ffn_W_r).T),
        "wkt": f32c(np.asarray(L0_ffn_W_k).T),
        "wvt": f32c(np.asarray(L0_ffn_W_v).T),
        "woutt": f32c(np.asarray(L0_attn_W_out).T),
        "wqa": dualT(L0_attn_Wq),
        "wka": dualT(L0_attn_Wk),
        "wva": dualT(L0_attn_Wv),
        "boutp": chv(L0_attn_b_out),
        "murh": chv(L0_ffn_mu_r),
        "mukh": chv(L0_ffn_mu_k),
    }
    in_maps = []
    for b in range(8):
        m = dict(base)
        m["xt"] = np.ascontiguousarray(x[b].T)
        in_maps.append(m)
    return in_maps


def kernel(**inputs):
    in_maps = make_in_maps(**inputs)
    nc = _get_nc()
    res = run_bass_kernel_spmd(nc, in_maps, core_ids=list(range(8)))
    out = np.empty((8, S, C), dtype=np.float32)
    for b in range(8):
        out[b] = res.results[b]["yt"].T
    return out


# version 22
# speedup vs baseline: 1.0335x; 1.0060x over previous
import numpy as np
from contextlib import ExitStack

import concourse.bass as bass
import concourse.mybir as mybir
from concourse.bass_utils import run_bass_kernel_spmd

F32 = mybir.dt.float32
BF16 = mybir.dt.bfloat16
I32 = mybir.dt.int32
AF = mybir.ActivationFunctionType
ALU = mybir.AluOpType
AX = mybir.AxisListType.X

C = 1024
S = 4096
NCH = 8
QC = 512
NQC = S // QC
D = 64
NPAIR = 8
KCN = 32
EPS = 1e-5
INV32 = 1.0 / 32.0

FINE_YIELDS = True
NO_OVERLAP = False
STRICT_KV = False
MAGIC_RSQRT = 0x5F3759DF
MAGIC_RECIP = 0x7EF477D5


class Sem:
    def __init__(self, h):
        self.h = h
        self.n = 0


class B:
    ENGS = ("sync", "tensor", "scalar", "vector", "gpsimd")

    def __init__(self):
        self.q = {e: [] for e in self.ENGS}
        self.free = {}

    def on(self, eng, fn):
        self.q[eng].append(fn)

    def wait(self, eng, sem, val):
        if val > 0:
            self.on(eng, lambda e, s=sem, v=val: e.wait_ge(s.h, v))

    def wf(self, eng, region):
        if region in self.free:
            sem, tick = self.free[region]
            self.wait(eng, sem, tick)

    def sf(self, region, sem, tick):
        self.free[region] = (sem, tick)

    def op(self, eng, sem, fn, inc=1):
        sem.n += inc
        self.on(eng, lambda e, f=fn, s=sem, i=inc: f(e).then_inc(s.h, i))
        return sem.n

    def raw(self, eng, fn):
        self.on(eng, fn)

    def dma(self, eng, sem, fn):
        sem.n += 16
        self.on(eng, lambda e, f=fn, s=sem: f(e).then_inc(s.h, 16))
        return sem.n

    def run(self, block):
        for eng in self.ENGS:
            fns = self.q[eng]
            def mk(fns):
                def body(e):
                    for f in fns:
                        f(e)
                return body
            getattr(block, eng)(mk(fns))


def build_nc(debug=False, nqc=NQC, do_b=True, stage='full'):
    nc = bass.Bass("TRN2", target_bir_lowering=False)

    xt = nc.dram_tensor("xt", [C, S], F32, kind="ExternalInput").ap()
    wrt = nc.dram_tensor("wrt", [C, C], F32, kind="ExternalInput").ap()
    wkt = nc.dram_tensor("wkt", [C, C], F32, kind="ExternalInput").ap()
    wvt = nc.dram_tensor("wvt", [C, C], F32, kind="ExternalInput").ap()
    woutt = nc.dram_tensor("woutt", [C, C], F32, kind="ExternalInput").ap()
    wqa = nc.dram_tensor("wqa", [128, D], F32, kind="ExternalInput").ap()
    wka = nc.dram_tensor("wka", [128, D], F32, kind="ExternalInput").ap()
    wva = nc.dram_tensor("wva", [128, D], F32, kind="ExternalInput").ap()
    bout = nc.dram_tensor("boutp", [128, NCH], F32, kind="ExternalInput").ap()
    murh = nc.dram_tensor("murh", [128, NCH], F32, kind="ExternalInput").ap()
    mukh = nc.dram_tensor("mukh", [128, NCH], F32, kind="ExternalInput").ap()
    yt = nc.dram_tensor("yt", [C, S], F32, kind="ExternalOutput").ap()
    scratch_kind = "ExternalOutput" if debug else "Internal"
    x1s = nc.dram_tensor("x1s", [C, S], F32, kind=scratch_kind).ap()
    dbgn = nc.dram_tensor("dbgn", [128, NPAIR], F32, kind=scratch_kind).ap()
    dbgr = nc.dram_tensor("dbgr", [8, QC], F32, kind=scratch_kind).ap()
    dbga = nc.dram_tensor("dbga", [128, S], BF16, kind=scratch_kind).ap()
    dbga2 = nc.dram_tensor("dbga2", [128, 1024], F32, kind=scratch_kind).ap()
    dbgp = nc.dram_tensor("dbgp", [128, KCN * 1024], BF16, kind=scratch_kind).ap()
    dbgs = nc.dram_tensor("dbgs", [2, 16], F32, kind=scratch_kind).ap()
    ks = nc.dram_tensor("ks", [NPAIR, 128, S], BF16, kind=scratch_kind).ap()
    vs = nc.dram_tensor("vs", [NPAIR, 128, KCN * 130], BF16, kind=scratch_kind).ap()
    xn1s = nc.dram_tensor("xn1s", [C, S], BF16, kind=scratch_kind).ap()

    with ExitStack() as ctx:
        en = ctx.enter_context
        wr_sb = en(nc.sbuf_tensor("wr_sb", [128, NCH * C], BF16))
        wk_sb = en(nc.sbuf_tensor("wk_sb", [128, NCH * C], BF16))
        wv_sb = en(nc.sbuf_tensor("wv_sb", [128, NCH * C], BF16))
        wout_sb = en(nc.sbuf_tensor("wout_sb", [128, NCH * C], BF16))
        wqa_sb = en(nc.sbuf_tensor("wqa_sb", [128, D], BF16))
        wka_sb = en(nc.sbuf_tensor("wka_sb", [128, D], BF16))
        wva_sb = en(nc.sbuf_tensor("wva_sb", [128, D], BF16))
        bout_sb = en(nc.sbuf_tensor("bout_sb", [128, NCH], F32))
        mur_sb = en(nc.sbuf_tensor("mur_sb", [128, NCH], F32))
        muk_sb = en(nc.sbuf_tensor("muk_sb", [128, NCH], F32))
        ones_sb = en(nc.sbuf_tensor("ones_sb", [128, 128], BF16))
        negc_sb = en(nc.sbuf_tensor("negc_sb", [128, NPAIR], F32))
        bnd1_sb = en(nc.sbuf_tensor("bnd1_sb", [128, NCH], BF16))
        bnd2_sb = en(nc.sbuf_tensor("bnd2_sb", [128, NCH], BF16))
        qmax_sb = en(nc.sbuf_tensor("qmax_sb", [1, 16], F32))
        kmax_sb = en(nc.sbuf_tensor("kmax_sb", [1, 16], F32))
        fr = en(nc.sbuf_tensor("fr", [1, 4 * QC], F32))
        ub64 = en(nc.sbuf_tensor("ub64", [128, 4 * QC], BF16))
        br = ub64
        avout = [en(nc.sbuf_tensor(f"avout{i}", [128, 2 * QC], F32)) for i in range(2)]
        xb = [en(nc.sbuf_tensor(f"xb{i}", [128, S], F32)) for i in range(2)]
        xn = en(nc.sbuf_tensor("xn", [128, NCH * (QC + 1)], BF16))
        xn1 = en(nc.sbuf_tensor("xn1", [128, S], BF16))
        xn1b = en(nc.sbuf_tensor("xn1b", [128, 4 * QC], BF16))
        b1 = en(nc.sbuf_tensor("b1", [128, S], BF16))
        b2_ = [en(nc.sbuf_tensor(f"b2_{i}", [128, S], BF16)) for i in range(2)]
        t0 = en(nc.sbuf_tensor("t0", [128, QC], F32))
        tb_ = [en(nc.sbuf_tensor(f"tb{i}", [128, QC], F32)) for i in range(2)]
        cast = [en(nc.sbuf_tensor(f"cast{i}", [128, 2 * QC], BF16)) for i in range(2)]
        stgk = [en(nc.sbuf_tensor(f"stgk{i}", [128, QC], BF16)) for i in range(2)]
        stgv = [en(nc.sbuf_tensor(f"stgv{i}", [128, 520], BF16)) for i in range(2)]
        kph = [en(nc.sbuf_tensor(f"kph{i}", [128, 16 * 128], BF16)) for i in range(3)]
        vph = [en(nc.sbuf_tensor(f"vph{i}", [128, 16 * 130], BF16)) for i in range(3)]
        ones_f32 = en(nc.sbuf_tensor("ones_f32", [128, 1], F32))
        pt = [en(nc.sbuf_tensor(f"pt{i}", [128, 1024], BF16)) for i in range(2)]
        q_sb = en(nc.sbuf_tensor("q_sb", [128, QC], BF16))

        pbig = en(nc.psum_tensor("pbig", [128, 2048], F32))
        pav = en(nc.psum_tensor("pav", [128, 1024], F32))
        pgen = en(nc.psum_tensor("pgen", [128, 1024], F32))

        P = B()
        T = Sem(en(nc.semaphore("T")))
        SS = Sem(en(nc.semaphore("SS")))
        V = Sem(en(nc.semaphore("V")))
        G = Sem(en(nc.semaphore("G")))
        d_x = [Sem(en(nc.semaphore(f"d_x{i}"))) for i in range(2)]
        d_sp = [Sem(en(nc.semaphore(f"d_sp{i}"))) for i in range(2)]
        d_k = [Sem(en(nc.semaphore(f"d_k{i}"))) for i in range(2)]
        d_v = [Sem(en(nc.semaphore(f"d_v{i}"))) for i in range(2)]
        d_gk = [Sem(en(nc.semaphore(f"d_gk{i}"))) for i in range(2)]
        d_gv = [Sem(en(nc.semaphore(f"d_gv{i}"))) for i in range(2)]
        d_w = Sem(en(nc.semaphore("d_w")))
        d_o = Sem(en(nc.semaphore("d_o")))
        d_n = Sem(en(nc.semaphore("d_n")))
        block = en(nc.Block())

        t_op = lambda fn: P.op("tensor", T, fn)
        s_op = lambda fn: P.op("scalar", SS, fn)
        v_op = lambda fn: P.op("vector", V, fn)
        g_op = lambda fn: P.op("gpsimd", G, fn)

        def hview(t_hbm, qc):
            return t_hbm[:, qc * QC:(qc + 1) * QC].rearrange("(r p) n -> p r n", p=128)

        def xt_like_xn1(qc):
            return xn1s[:, qc * QC:(qc + 1) * QC].rearrange("(r p) n -> p r n", p=128)

        def xv(i):
            return xb[i][:].rearrange("p (r n) -> p r n", r=NCH)

        def wslice(sb, kc, m):
            return sb[:, kc * C + m * 128: kc * C + (m + 1) * 128]

        def xn_c(r):
            return xn[:, r * (QC + 1) + 1: r * (QC + 1) + 1 + QC]

        def xn_p(r):
            return xn[:, r * (QC + 1): r * (QC + 1) + QC]

        r_mean = fr[0:1, 0 * QC:1 * QC]
        r_var = fr[0:1, 1 * QC:2 * QC]
        r_rstd = fr[0:1, 2 * QC:3 * QC]
        r_t = fr[0:1, 3 * QC:4 * QC]
        r_nmr = r_mean
        b_rhi = br[0:1, 0 * QC:1 * QC]
        b_rlo = br[0:1, 1 * QC:2 * QC]
        b_nhi = br[0:1, 2 * QC:3 * QC]
        b_nlo = br[0:1, 3 * QC:4 * QC]
        qbufs = (q_sb, stgk[0])

        P.dma("sync", d_x[0], lambda e: e.dma_start(out=xv(0), in_=hview(xt, 0)))
        P.dma("sync", d_x[1], lambda e: e.dma_start(out=xv(1), in_=hview(xt, 1)))
        P.dma("sync", d_w, lambda e: e.dma_start(out=bout_sb[:], in_=bout))
        P.dma("sync", d_w, lambda e: e.dma_start(out=mur_sb[:], in_=murh))
        P.dma("sync", d_w, lambda e: e.dma_start(out=muk_sb[:], in_=mukh))

        def wv_(sb):
            return sb[:].rearrange("p (kc d) -> p kc d", kc=NCH)

        def wh_(t_hbm):
            return t_hbm.rearrange("(kc p) d -> p kc d", p=128)

        for sb_t, hb in ((wr_sb, wrt), (wk_sb, wkt), (wv_sb, wvt), (wout_sb, woutt)):
            P.dma("gpsimd", d_w, lambda e, sb_t=sb_t, hb=hb: e.dma_start(
                out=wv_(sb_t), in_=wh_(hb)))
        for sb_t, hb in ((wqa_sb, wqa), (wka_sb, wka), (wva_sb, wva)):
            P.dma("gpsimd", d_w, lambda e, sb_t=sb_t, hb=hb: e.dma_start(
                out=sb_t[:], in_=hb))
        w_tick = d_w.n

        g_op(lambda e: e.memset(ones_sb[:], 1.0))
        g_op(lambda e: e.memset(ones_f32[:], 1.0))
        g_op(lambda e: e.memset(bnd1_sb[:], 0.0))
        g_op(lambda e: e.memset(bnd2_sb[:], 0.0))
        g_op(lambda e: e.memset(qmax_sb[:], 0.0))
        g_op(lambda e: e.memset(kmax_sb[:], 0.0))
        g_op(lambda e: e.memset(stgv[0][:], 1.0))
        init_tick = g_op(lambda e: e.memset(stgv[1][:], 1.0))

        P.wait("vector", G, init_tick)
        P.wait("tensor", G, init_tick)
        P.wait("tensor", d_w, w_tick)
        P.wait("scalar", G, init_tick)

        def newton_rsqrt(vrow, yrow, trow):
            irow = trow.bitcast(I32)
            v_op(lambda e: e.tensor_single_scalar(irow, vrow.bitcast(I32), 1,
                                                  ALU.logical_shift_right))
            v_op(lambda e: e.tensor_scalar(irow, irow, -1, MAGIC_RSQRT, ALU.mult, ALU.add))
            v_op(lambda e: e.tensor_copy(yrow, irow.bitcast(F32)))
            for _ in range(2):
                v_op(lambda e: e.tensor_tensor(trow, yrow, yrow, ALU.mult))
                v_op(lambda e: e.tensor_tensor(trow, trow, vrow, ALU.mult))
                v_op(lambda e: e.tensor_scalar(trow, trow, -0.5, 1.5, ALU.mult, ALU.add))
                v_op(lambda e: e.tensor_tensor(yrow, yrow, trow, ALU.mult))

        def newton_recip(vrow, yrow, trow):
            irow = trow.bitcast(I32)
            v_op(lambda e: e.tensor_scalar(irow, vrow.bitcast(I32), -1, MAGIC_RECIP,
                                           ALU.mult, ALU.add))
            v_op(lambda e: e.tensor_copy(yrow, irow.bitcast(F32)))
            for _ in range(2):
                v_op(lambda e: e.tensor_tensor(trow, vrow, yrow, ALU.mult))
                v_op(lambda e: e.tensor_scalar(trow, trow, -1.0, 2.0, ALU.mult, ALU.add))
                v_op(lambda e: e.tensor_tensor(yrow, yrow, trow, ALU.mult))

        def ln_stat_chunk(sv, r, deps=()):
            cb = cast[r % 2]
            for (sem, tick) in deps:
                P.wait("scalar", sem, tick)
            P.wf("scalar", f"cast{r % 2}")
            if r == 0:
                P.wf("tensor", "pg6")
                P.wf("tensor", "pg7")
            s_op(lambda e, r=r, cb=cb: e.activation(
                cb[:, 0:QC], sv[:, r, :], AF.Copy, bias=0.0, scale=1.0))
            st = s_op(lambda e, r=r, cb=cb: e.activation(
                cb[:, QC:2 * QC], sv[:, r, :], AF.Square, bias=0.0, scale=1.0))
            P.wait("tensor", SS, st)
            t_op(lambda e, cb=cb, r=r: e.matmul(
                pgen[0:1, 0:QC], ones_sb[:, 0:1], cb[:, 0:QC],
                start=(r == 0), stop=(r == NCH - 1)))
            m2 = t_op(lambda e, cb=cb, r=r: e.matmul(
                pgen[0:1, QC:2 * QC], ones_sb[:, 0:1], cb[:, QC:2 * QC],
                start=(r == 0), stop=(r == NCH - 1)))
            P.sf(f"cast{r % 2}", T, m2)
            return m2

        def ln_rows(last_mm):
            P.wait("vector", T, last_mm)
            v_op(lambda e: e.tensor_single_scalar(r_mean, pgen[0:1, 0:QC], 1.0 / C, ALU.mult))
            v_op(lambda e: e.tensor_single_scalar(r_var, pgen[0:1, QC:2 * QC], 1.0 / C, ALU.mult))
            v_op(lambda e: e.tensor_tensor(r_t, r_mean, r_mean, ALU.mult))
            v_op(lambda e: e.tensor_tensor(r_var, r_var, r_t, ALU.subtract))
            v_op(lambda e: e.tensor_scalar_add(r_var, r_var, EPS))
            newton_rsqrt(r_var, r_rstd, r_t)
            v_op(lambda e: e.scalar_tensor_tensor(r_nmr, r_mean, -1.0, r_rstd,
                                                  ALU.mult, ALU.mult))
            v_op(lambda e: e.tensor_copy(b_rhi, r_rstd))
            v_op(lambda e: e.tensor_tensor(b_rlo, r_rstd, b_rhi, ALU.subtract))
            v_op(lambda e: e.tensor_copy(b_nhi, r_nmr))
            return v_op(lambda e: e.tensor_tensor(b_nlo, r_nmr, b_nhi, ALU.subtract))

        def ln_reps(hl, target):
            if target == 'pgen':
                rsl, nsl = pgen[:, 0:QC], pgen[:, QC:2 * QC]
                k1, k2 = "pg6", "pg7"
            else:
                rsl, nsl = pav[:, 0:QC], pav[:, QC:2 * QC]
                k1, k2 = "pavH0", "pavH1"
            P.wf("tensor", k1)
            P.wf("tensor", k2)
            P.wait("tensor", V, hl)
            t_op(lambda e: e.matmul(rsl, ones_sb[0:1, :], b_rhi, start=True, stop=False))
            t_op(lambda e: e.matmul(rsl, ones_sb[0:1, :], b_rlo, start=False, stop=True))
            t_op(lambda e: e.matmul(nsl, ones_sb[0:1, :], b_nhi, start=True, stop=False))
            rep = t_op(lambda e: e.matmul(nsl, ones_sb[0:1, :], b_nlo, start=False, stop=True))
            return rep, rsl, nsl, k1, k2

        def ln_norm(sv, dst_fn, rep, rsl, nsl, k1, k2, pre=()):
            P.wait("vector", T, rep)
            for (sem, tick) in pre:
                P.wait("vector", sem, tick)
            last = 0
            for r in range(NCH):
                v_op(lambda e, r=r: e.tensor_tensor(
                    t0[:], sv[:, r, :], rsl, ALU.mult))
                last = v_op(lambda e, r=r: e.tensor_tensor(
                    dst_fn(r), t0[:], nsl, ALU.add))
            P.sf(k1, V, last)
            P.sf(k2, V, last)
            return last

        def ln(src, dst_fn, deps, target='pgen', pre=()):
            sv = src[:].rearrange("p (r n) -> p r n", r=NCH)
            last_mm = 0
            for r in range(NCH):
                last_mm = ln_stat_chunk(sv, r, deps if r == 0 else ())
            hl = ln_rows(last_mm)
            if target != 'pgen':
                P.sf("pg6", V, hl)
                P.sf("pg7", V, hl)
            rep, rsl, nsl, k1, k2 = ln_reps(hl, target)
            return ln_norm(sv, dst_fn, rep, rsl, nsl, k1, k2, pre)

        def boundary_restore(bnd):
            last = 0
            for r in range(NCH):
                last = v_op(lambda e, r=r, bnd=bnd: e.tensor_copy(
                    xn[:, r * (QC + 1): r * (QC + 1) + 1], bnd[:, r:r + 1]))
            return last

        def ffn_gen(src, bnd, b2buf, b2reg, banks, res):
            nb = len(banks)
            sv = src[:].rearrange("p (r n) -> p r n", r=NCH)
            b1v = b1[:].rearrange("p (r n) -> p r n", r=NCH)
            b2v = b2buf[:].rearrange("p (r n) -> p r n", r=NCH)
            xr_t = 0
            lerp_ts = []
            xr_ts = []
            for r in range(NCH):
                if r == 0:
                    P.wf("vector", "b1")
                    P.wf("gpsimd", b2reg)
                dt_ = v_op(lambda e, r=r: e.tensor_tensor(b1v[:, r, :], xn_c(r), xn_p(r),
                                                          ALU.subtract))
                if r == 0:
                    P.wait("gpsimd", V, dt_)
                g_op(lambda e, r=r, b2v=b2v: e.tensor_tensor(b2v[:, r, :], xn_c(r), xn_p(r),
                                                    ALU.subtract))
                g_op(lambda e, r=r, b2v=b2v: e.tensor_tensor(
                    b2v[:, r, :], b2v[:, r, :],
                    mur_sb[:, r:r + 1].broadcast_to([128, QC]), ALU.mult))
                xr_t = g_op(lambda e, r=r, b2v=b2v: e.tensor_tensor(
                    b2v[:, r, :], b2v[:, r, :], xn_p(r), ALU.add))
                xr_ts.append(xr_t)
                lerp_t = v_op(lambda e, r=r: e.scalar_tensor_tensor(
                    b1v[:, r, :], b1v[:, r, :], muk_sb[:, r:r + 1], xn_p(r),
                    ALU.mult, ALU.add))
                lerp_ts.append(lerp_t)
                if r % 2 == 1:
                    yield "c"
            for r in range(NCH):
                v_op(lambda e, r=r, bnd=bnd: e.tensor_copy(
                    bnd[:, r:r + 1], xn[:, r * (QC + 1) + QC: r * (QC + 1) + QC + 1]))
            yield "c"
            kact_first = True
            kact_t = 0
            kact_ts = []
            for m in range(NCH):
                pb, pbreg = banks[m % nb]
                P.wf("tensor", pbreg)
                mt = 0
                for kc in range(NCH):
                    if m == 0:
                        P.wait("tensor", V, lerp_ts[kc])
                    mt = t_op(lambda e, m=m, kc=kc, pb=pb: e.matmul(
                        pb, wslice(wk_sb, kc, m), b1v[:, kc, :],
                        start=(kc == 0), stop=(kc == NCH - 1)))
                    if FINE_YIELDS and kc < NCH - 1:
                        yield "f"
                tb = tb_[m % 2]
                P.wait("scalar", T, mt)
                P.wf("scalar", f"tb{m % 2}")
                rl = s_op(lambda e, pb=pb, tb=tb: e.activation(
                    tb[:], pb, AF.Relu, bias=0.0, scale=1.0))
                P.sf(pbreg, SS, rl)
                P.wait("vector", SS, rl)
                if kact_first:
                    P.wait("vector", G, xr_t)
                    kact_first = False
                kact_t = v_op(lambda e, m=m, tb=tb: e.tensor_tensor(
                    xn_c(m), tb[:], tb[:], ALU.mult))
                kact_ts.append(kact_t)
                P.sf(f"tb{m % 2}", V, kact_t)
                yield "c"
            out_t = 0
            out_ts = []
            for m in range(NCH):
                pr, prreg = banks[m % 2]
                pv, pvreg = banks[2 + (m + 1) % 2 if nb == 4 else (m + 1) % 2]
                P.wf("tensor", prreg)
                mt = 0
                for kc in range(NCH):
                    if m == 0:
                        P.wait("tensor", G, xr_ts[kc])
                    mt = t_op(lambda e, m=m, kc=kc, pr=pr, b2v=b2v: e.matmul(
                        pr, wslice(wr_sb, kc, m), b2v[:, kc, :],
                        start=(kc == 0), stop=(kc == NCH - 1)))
                    if FINE_YIELDS and kc < NCH - 1:
                        yield "f"
                tb = tb_[m % 2]
                P.wait("scalar", T, mt)
                P.wf("scalar", f"tb{m % 2}")
                th = s_op(lambda e, pr=pr, tb=tb: e.activation(
                    tb[:], pr, AF.Tanh, bias=0.0, scale=0.5))
                P.sf(prreg, SS, th)
                P.wf("tensor", pvreg)
                if FINE_YIELDS:
                    yield "f"
                mtv = 0
                for kc in range(NCH):
                    if m == 0:
                        P.wait("tensor", V, kact_ts[kc])
                    mtv = t_op(lambda e, m=m, kc=kc, pv=pv: e.matmul(
                        pv, wslice(wv_sb, kc, m), xn_c(kc),
                        start=(kc == 0), stop=(kc == NCH - 1)))
                    if FINE_YIELDS and kc < NCH - 1:
                        yield "f"
                P.wait("vector", SS, th)
                P.wait("vector", T, mtv)
                ct = v_op(lambda e, tb=tb, pv=pv: e.scalar_tensor_tensor(
                    t0[:], tb[:], 1.0, pv, ALU.add, ALU.mult))
                P.sf(f"tb{m % 2}", V, ct)
                P.sf(pvreg, V, ct)
                out_t = v_op(lambda e, m=m, sv=sv: e.scalar_tensor_tensor(
                    sv[:, m, :], t0[:], 0.25, sv[:, m, :], ALU.mult, ALU.add))
                out_ts.append(out_t)
                yield "c"
            P.sf("b1", V, out_t)
            P.sf(b2reg, V, out_t)
            P.sf("xnreg", T, mtv)
            res['out_t'] = out_t
            res['out_ts'] = out_ts

        PB_BANKS = [(pbig[:, i * QC:(i + 1) * QC], f"pb{i}") for i in range(4)]
        PG_BANKS = [(pgen[:, 0:QC], "pg6"), (pgen[:, QC:2 * QC], "pg7")]
        spill_tick = [0, 0]
        a_n2t = [0, 0]
        a_spill_xn1 = []

        def ln1_emit(qc_):
            b_ = qc_ % 2
            pre = [P.free["xnreg"]] if "xnreg" in P.free else []
            return ln(xb[b_], xn_c, [(d_x[b_], 16 * (qc_ // 2 + 1))],
                      target='pav', pre=pre)

        ln1_emit(0)
        for qc in range(nqc):
            buf = qc % 2
            src = xb[buf]
            boundary_restore(bnd1_sb)
            fres = {}
            for _ in ffn_gen(src, bnd1_sb, b2_[buf], f"b2{buf}", PB_BANKS, fres):
                pass
            ffn_t, ffn_ts = fres['out_t'], fres['out_ts']
            P.wait("scalar", V, ffn_t)
            sp = P.dma("scalar", d_sp[buf], lambda e, qc=qc, src=src: e.dma_start(
                out=hview(x1s, qc), in_=src[:].rearrange("p (r n) -> p r n", r=NCH)))
            spill_tick[buf] = sp
            if stage == 'ffn':
                continue
            xn1v = xn1[:].rearrange("p (r n) -> p r n", r=NCH)
            sv2 = src[:].rearrange("p (r n) -> p r n", r=NCH)
            lm2 = 0
            for m in range(NCH):
                lm2 = ln_stat_chunk(sv2, m, deps=[(V, ffn_ts[m])])
            hl2_ = ln_rows(lm2)
            rep2, rsl2, nsl2, k12, k22 = ln_reps(hl2_, 'pgen')
            P.wf("vector", "xn1w")
            P.wf("vector", "xn1r")
            n2t = ln_norm(sv2, lambda r: xn1v[:, r, :], rep2, rsl2, nsl2, k12, k22)
            a_n2t[buf] = n2t
            if qc + 1 < nqc:
                ln1_emit(qc + 1)
            P.wait("scalar", V, n2t)
            xn1_spill = P.dma("scalar", d_n, lambda e, qc=qc: e.dma_start(
                out=xt_like_xn1(qc),
                in_=xn1[:].rearrange("p (r n) -> p r n", r=NCH)))
            P.sf("xn1w", d_n, xn1_spill)
            a_spill_xn1.append(xn1_spill)
            last_q_mm = 0

            def kv_tail(c):
                p_, kb, qb, kmt_, qmt_ = c
                pairch = xn1v[:, p_, :]
                sg = p_ % 2
                kslice = pav[:, kb * QC:(kb + 1) * QC]
                qslice = pbig[:, qb * QC:(qb + 1) * QC]
                P.wait("scalar", T, kmt_)
                P.wait("scalar", d_gk[sg], d_gk[sg].n)
                ke = s_op(lambda e, sg=sg, kslice=kslice: e.activation(
                    stgk[sg][:], kslice, AF.Copy, bias=0.0, scale=1.0))
                P.wait("scalar", SS, ke)
                P.dma("scalar", d_gk[sg], lambda e, p_=p_, qc=qc, sg=sg: e.dma_start(
                    out=ks[p_, :, qc * QC:(qc + 1) * QC], in_=stgk[sg][:]))
                P.wf("scalar", "cast0")
                ksq = s_op(lambda e, kslice=kslice: e.activation(
                    cast[0][:, QC:2 * QC], kslice, AF.Square, bias=0.0, scale=1.0))
                P.sf(f"pavH{kb}", SS, ksq)
                P.wait("tensor", SS, ksq)
                P.wf("tensor", "pg6")
                ksm = t_op(lambda e: e.matmul(pgen[0:1, 0:QC], ones_sb[:, 0:1],
                                              cast[0][:, QC:2 * QC], start=True, stop=True))
                P.sf("cast0", T, ksm)
                P.wait("vector", T, ksm)
                krd = v_op(lambda e: e.reduce_max(r_t[0:1, 0:1], pgen[0:1, 0:QC], axis=AX))
                P.wait("vector", V, krd)
                kred = v_op(lambda e, p_=p_: e.tensor_tensor(
                    kmax_sb[0:1, p_:p_ + 1], kmax_sb[0:1, p_:p_ + 1], r_t[0:1, 0:1], ALU.max))
                P.sf("pg6", V, kred)
                P.wait("scalar", T, qmt_)
                P.wf("scalar", "cast1")
                qsq = s_op(lambda e, qslice=qslice: e.activation(
                    cast[1][:, QC:2 * QC], qslice, AF.Square, bias=0.0, scale=INV32))
                P.sf(f"pb{qb}", SS, qsq)
                P.wait("tensor", SS, qsq)
                P.wf("tensor", "pg7")
                qsm = t_op(lambda e: e.matmul(pgen[0:1, QC:2 * QC], ones_sb[:, 0:1],
                                              cast[1][:, QC:2 * QC], start=True, stop=True))
                P.sf("cast1", T, qsm)
                P.wait("vector", T, qsm)
                qrd = v_op(lambda e: e.reduce_max(r_t[0:1, 1:2], pgen[0:1, QC:2 * QC], axis=AX))
                P.wait("vector", V, qrd)
                qred = v_op(lambda e, p_=p_: e.tensor_tensor(
                    qmax_sb[0:1, p_:p_ + 1], qmax_sb[0:1, p_:p_ + 1], r_t[0:1, 1:2], ALU.max))
                P.sf("pg7", V, qred)
                P.wf("tensor", "pb0")
                P.wf("tensor", "pb1")
                vmt_ = 0
                for tc in range(4):
                    lo = tc * 128
                    P.raw("tensor", lambda e, pairch=pairch, lo=lo, tc=tc: e.matmul(
                        pbig[:, tc * 64: tc * 64 + 64], pairch[0:64, lo:lo + 128],
                        wva_sb[0:64, :], start=True, stop=True, tile_position=(0, 0)))
                    vmt_ = t_op(lambda e, pairch=pairch, lo=lo, tc=tc: e.matmul(
                        pbig[:, QC + tc * 64: QC + tc * 64 + 64], pairch[64:128, lo:lo + 128],
                        wva_sb[64:128, :], start=True, stop=True, tile_position=(64, 0)))
                sgv = p_ % 2
                P.wait("vector", T, vmt_)
                P.wait("vector", d_gv[sgv], d_gv[sgv].n)
                stv = stgv[sgv][:].rearrange("q (t c) -> q t c", c=130)
                pa = pbig[:, 0:256].rearrange("q (t c) -> q t c", c=64)
                pb_ = pbig[:, QC:QC + 256].rearrange("q (t c) -> q t c", c=64)
                v_op(lambda e, stv=stv, pa=pa: e.tensor_copy(stv[:, :, 0:64], pa))
                vev = v_op(lambda e, stv=stv, pb_=pb_: e.tensor_copy(stv[:, :, 65:129], pb_))
                P.sf("pb0", V, vev)
                P.sf("pb1", V, vev)
                P.wait("scalar", V, vev)
                P.dma("scalar", d_gv[sgv], lambda e, p_=p_, qc=qc, sgv=sgv: e.dma_start(
                    out=vs[p_, :, qc * 520:(qc + 1) * 520], in_=stgv[sgv][:]))
                return vmt_

            prevkv = None
            for p in range(NPAIR):
                pairch = xn1v[:, p, :]
                kb = p % 2
                qb = 2 + p % 2
                P.wf("tensor", f"pavH{kb}")
                if p == 0:
                    P.wait("tensor", V, n2t)
                P.raw("tensor", lambda e, pairch=pairch, kb=kb: e.matmul(
                    pav[0:64, kb * QC:kb * QC + QC], wka_sb[0:64, :], pairch[0:64, :],
                    start=True, stop=True, tile_position=(0, 0)))
                kmt = t_op(lambda e, pairch=pairch, kb=kb: e.matmul(
                    pav[64:128, kb * QC:kb * QC + QC], wka_sb[64:128, :], pairch[64:128, :],
                    start=True, stop=True, tile_position=(64, 64)))
                P.wf("tensor", f"pb{qb}")
                P.raw("tensor", lambda e, pairch=pairch, qb=qb: e.matmul(
                    pbig[0:64, qb * QC:qb * QC + QC], wqa_sb[0:64, :], pairch[0:64, :],
                    start=True, stop=True, tile_position=(0, 0)))
                qmt = t_op(lambda e, pairch=pairch, qb=qb: e.matmul(
                    pbig[64:128, qb * QC:qb * QC + QC], wqa_sb[64:128, :], pairch[64:128, :],
                    start=True, stop=True, tile_position=(64, 64)))
                last_q_mm = qmt
                if prevkv is not None:
                    kv_tail(prevkv)
                prevkv = (p, kb, qb, kmt, qmt)
            last_vmt = kv_tail(prevkv)
            last_q_mm = max(last_q_mm, last_vmt)
            P.sf("xn1r", T, last_q_mm)
            if qc + 2 < nqc:
                P.wait("sync", d_sp[buf], spill_tick[buf])
                P.wait("sync", V, n2t)
                P.dma("sync", d_x[buf], lambda e, qc=qc, buf=buf: e.dma_start(
                    out=xv(buf), in_=hview(xt, qc + 2)))

        t1_ = v_op(lambda e: e.tensor_tensor(r_t[0:1, 0:NPAIR], qmax_sb[0:1, 0:NPAIR],
                                             kmax_sb[0:1, 0:NPAIR], ALU.mult))
        P.wait("vector", V, t1_)
        t2_ = v_op(lambda e: e.tensor_scalar(r_t[0:1, 0:NPAIR], r_t[0:1, 0:NPAIR], 0.51, 1.0,
                                             ALU.mult, ALU.add))
        P.wait("vector", V, t2_)
        t3_ = v_op(lambda e: e.tensor_copy(br[0:1, 0:NPAIR], r_t[0:1, 0:NPAIR]))
        P.wait("vector", V, t3_)
        chl = v_op(lambda e: e.tensor_tensor(br[0:1, NPAIR:2 * NPAIR],
                                             r_t[0:1, 0:NPAIR], br[0:1, 0:NPAIR],
                                             ALU.subtract))
        P.wf("tensor", "pg6")
        P.wait("tensor", V, chl)
        t_op(lambda e: e.matmul(pgen[:, 0:NPAIR], ones_sb[0:1, :], br[0:1, 0:NPAIR],
                                start=True, stop=False))
        cm = t_op(lambda e: e.matmul(pgen[:, 0:NPAIR], ones_sb[0:1, :],
                                     br[0:1, NPAIR:2 * NPAIR], start=False, stop=True))
        P.wait("scalar", T, cm)
        negc_t = s_op(lambda e: e.activation(negc_sb[:], pgen[:, 0:NPAIR], AF.Copy,
                                             bias=0.0, scale=-1.0))
        P.sf("pg6", SS, negc_t)
        if debug:
            P.wait("scalar", SS, negc_t)
            P.dma("scalar", d_w, lambda e: e.dma_start(out=dbgn, in_=negc_sb[:]))
            P.dma("scalar", d_w, lambda e: e.dma_start(out=dbgs[0:1, :], in_=qmax_sb[:]))
            P.dma("scalar", d_w, lambda e: e.dma_start(out=dbgs[1:2, :], in_=kmax_sb[:]))

        if not do_b:
            P.wait("sync", d_sp[0], d_sp[0].n)
            P.wait("sync", d_sp[1], d_sp[1].n)
            for s_ in (d_gk[0], d_gk[1], d_gv[0], d_gv[1]):
                P.wait("sync", s_, s_.n)
            P.run(block)
            return nc
        P.wait("sync", d_sp[0], spill_tick[0])
        P.wait("sync", V, a_n2t[0])
        ld0 = P.dma("sync", d_x[0], lambda e: e.dma_start(out=xv(0), in_=hview(x1s, 0)))
        P.wait("sync", d_sp[1], spill_tick[1])
        P.wait("sync", V, a_n2t[1])
        ld1 = P.dma("sync", d_x[1], lambda e: e.dma_start(out=xv(1), in_=hview(x1s, 1)))
        x_load_tick = [ld0, ld1]
        for s_ in (d_gk[0], d_gk[1], d_gv[0], d_gv[1]):
            P.wait("sync", s_, s_.n)
        NH = 2 * nqc * NPAIR
        kht = [0] * NH
        vht = [0] * NH
        kfree = [0] * NH
        vfree = [0] * NH

        def kv_load(h):
            pr_ = (h // 2) % NPAIR
            hb = (h % 2)
            kht[h] = P.dma("sync", d_k[0], lambda e, h=h, pr_=pr_, hb=hb: e.dma_start(
                out=kph[h % 3][:], in_=ks[pr_, :, hb * 2048:(hb + 1) * 2048]))
            vht[h] = P.dma("sync", d_v[0], lambda e, h=h, pr_=pr_, hb=hb: e.dma_start(
                out=vph[h % 3][:], in_=vs[pr_, :, hb * 2080:(hb + 1) * 2080]))

        for h0 in range(3):
            kv_load(h0)
        P.wait("tensor", d_k[0], kht[2])
        P.wait("tensor", d_v[0], vht[2])
        pt_guard = [0, 0]

        def norm_rows(ab2):
            u_y = avout[1 - ab2][64:65, 0:QC]
            u_t = avout[1 - ab2][64:65, QC:2 * QC]
            hl2 = 0
            for hh in range(2):
                vrow = avout[ab2][64:65, hh * QC:(hh + 1) * QC]
                newton_recip(vrow, u_y, u_t)
                v_op(lambda e, hh=hh, u_y=u_y: e.tensor_copy(
                    ub64[64:65, (2 * hh) * QC:(2 * hh + 1) * QC], u_y))
                hl2 = v_op(lambda e, hh=hh, u_y=u_y: e.tensor_tensor(
                    ub64[64:65, (2 * hh + 1) * QC:(2 * hh + 2) * QC], u_y,
                    ub64[64:65, (2 * hh) * QC:(2 * hh + 1) * QC], ALU.subtract))
            return hl2

        odo_tick = [0, 0]

        def norm_apply(pp, ab2, hl2, b2vq, b2q):
            sgo = pp % 2
            P.wf("tensor", "pg7")
            P.wait("tensor", V, hl2)
            P.raw("tensor", lambda e: e.matmul(
                pgen[0:64, QC:2 * QC], ones_sb[64:65, 0:64],
                ub64[64:65, 0:QC], start=True, stop=False, tile_position=(64, 0)))
            P.raw("tensor", lambda e: e.matmul(
                pgen[0:64, QC:2 * QC], ones_sb[64:65, 0:64],
                ub64[64:65, QC:2 * QC], start=False, stop=True,
                tile_position=(64, 0)))
            P.wf("tensor", "pg6")
            P.raw("tensor", lambda e: e.matmul(
                pgen[0:64, 0:QC], ones_sb[64:65, 0:64],
                ub64[64:65, 2 * QC:3 * QC], start=True, stop=False,
                tile_position=(64, 0)))
            rpt = t_op(lambda e: e.matmul(
                pgen[0:64, 0:QC], ones_sb[64:65, 0:64],
                ub64[64:65, 3 * QC:4 * QC], start=False, stop=True,
                tile_position=(64, 0)))
            P.wait("vector", T, rpt)
            aA = v_op(lambda e, pp=pp, ab2=ab2, b2vq=b2vq: e.tensor_tensor(
                b2vq[0:64, pp, :], avout[ab2][0:64, 0:QC],
                pgen[0:64, QC:2 * QC], ALU.mult))
            P.sf("pg7", V, aA)
            P.wait("vector", d_o, odo_tick[sgo])
            last = v_op(lambda e, ab2=ab2, sgo=sgo: e.tensor_tensor(
                stgv[sgo][0:64, 0:QC], avout[ab2][0:64, QC:2 * QC],
                pgen[0:64, 0:QC], ALU.mult))
            P.sf("pg6", V, last)
            P.sf(f"avout{ab2}", V, last)
            P.wait("gpsimd", V, last)
            odo_tick[sgo] = P.dma("gpsimd", d_o, lambda e, pp=pp, sgo=sgo, b2q=b2q:
                e.dma_start(out=b2q[64:128, pp * QC:(pp + 1) * QC],
                            in_=stgv[sgo][0:64, 0:QC]))
            return last

        NSLOT = 3
        slot_tick = [0] * NSLOT
        slot_qmt = [0] * NSLOT

        def xn1b_slot(g):
            s = g % NSLOT
            return xn1b[:, s * QC:(s + 1) * QC]

        def ring_load(g):
            s = g % NSLOT
            qc_, p_ = g // NPAIR, g % NPAIR
            P.wait("sync", d_n, a_spill_xn1[qc_])
            P.wait("sync", T, slot_qmt[s])
            slot_tick[s] = P.dma("sync", d_n, lambda e, g=g, qc_=qc_, p_=p_:
                e.dma_start(out=xn1b_slot(g),
                            in_=xn1s[128 * p_:128 * (p_ + 1),
                                     qc_ * QC:(qc_ + 1) * QC]))

        for g0 in range(NSLOT - 1):
            ring_load(g0)
        P.wait("vector", d_gv[0], d_gv[0].n)
        P.wait("vector", d_gv[1], d_gv[1].n)
        P.wait("vector", d_gk[1], d_gk[1].n)
        P.sf("qb1", d_gk[0], d_gk[0].n)
        def epilogue(eqc, enorm_t, eod):
            ebuf = eqc % 2
            esrc = xb[ebuf]
            eb2v = b2_[ebuf][:].rearrange("p (r n) -> p r n", r=NCH)
            xsv = esrc[:].rearrange("p (r n) -> p r n", r=NCH)
            s1 = t0[:]
            s2 = cast[0][:].bitcast(F32)[:, 0:QC]
            sq = cast[1][:].bitcast(F32)[:, 0:QC]
            P.wait("tensor", V, enorm_t)
            P.wait("tensor", d_o, eod)
            P.wait("tensor", d_x[ebuf], x_load_tick[ebuf])
            s_t = 0
            for m in range(NCH):
                pb, pbreg = PG_BANKS[m % 2]
                P.wf("tensor", pbreg)
                mt = 0
                for kc in range(NCH):
                    mt = t_op(lambda e, m=m, kc=kc, pb=pb, eb2v=eb2v: e.matmul(
                        pb, wslice(wout_sb, kc, m), eb2v[:, kc, :],
                        start=(kc == 0), stop=(kc == NCH - 1)))
                    if FINE_YIELDS and kc < NCH - 1:
                        yield "f"
                tb = tb_[m % 2]
                P.wait("vector", T, mt)
                P.wf("vector", f"tb{m % 2}")
                v_op(lambda e, pb=pb, tb=tb, m=m: e.tensor_tensor(
                    tb[:], pb, bout_sb[:, m:m + 1].broadcast_to([128, QC]), ALU.add))
                proj_t = v_op(lambda e, m=m, tb=tb, xsv=xsv: e.tensor_tensor(
                    xsv[:, m, :], xsv[:, m, :], tb[:], ALU.add))
                P.sf(pbreg, V, proj_t)
                P.sf(f"tb{m % 2}", V, proj_t)
                if m == 0:
                    v_op(lambda e, s1=s1, xsv=xsv: e.tensor_copy(s1, xsv[:, 0, :]))
                    s_t = v_op(lambda e, s2=s2, xsv=xsv: e.tensor_tensor(
                        s2, xsv[:, 0, :], xsv[:, 0, :], ALU.mult))
                else:
                    v_op(lambda e, m=m, s1=s1, xsv=xsv: e.tensor_tensor(
                        s1, s1, xsv[:, m, :], ALU.add))
                    v_op(lambda e, m=m, sq=sq, xsv=xsv: e.tensor_tensor(
                        sq, xsv[:, m, :], xsv[:, m, :], ALU.mult))
                    s_t = v_op(lambda e, s2=s2, sq=sq: e.tensor_tensor(
                        s2, s2, sq, ALU.add))
                if m == NCH - 1:
                    P.sf(f"b2{ebuf}", T, mt)
                yield "c"
            if stage == 'bx2':
                if eqc == 0:
                    P.wait("scalar", V, enorm_t)
                    P.wait("scalar", d_o, eod)
                    P.dma("scalar", d_w, lambda e: e.dma_start(
                        out=dbga, in_=b2_[0][:]))
                P.wait("scalar", V, s_t)
                yst = P.dma("scalar", d_sp[ebuf], lambda e, eqc=eqc, esrc=esrc:
                    e.dma_start(out=hview(yt, eqc),
                                in_=esrc[:].rearrange("p (r n) -> p r n", r=NCH)))
                if eqc + 2 < nqc:
                    P.wait("sync", d_sp[ebuf], yst)
                    x_load_tick[ebuf] = P.dma("sync", d_x[ebuf],
                        lambda e, eqc=eqc, ebuf=ebuf: e.dma_start(
                            out=xv(ebuf), in_=hview(x1s, eqc + 2)))
                return
            yield "b"
            s1b = cast[1][:, 0:QC]
            s2hi = cast[1][:, QC:2 * QC]
            s2lo = stgk[1][:, 0:QC]
            v_op(lambda e, s1=s1, s1b=s1b: e.tensor_copy(s1b, s1))
            v_op(lambda e, s2=s2, s2hi=s2hi: e.tensor_copy(s2hi, s2))
            s_t = v_op(lambda e, s2=s2, s2hi=s2hi, s2lo=s2lo: e.tensor_tensor(
                s2lo, s2, s2hi, ALU.subtract))
            P.wf("tensor", "pg6")
            P.wf("tensor", "pg7")
            P.wait("tensor", V, s_t)
            t_op(lambda e, s1b=s1b: e.matmul(pgen[0:1, 0:QC], ones_sb[:, 0:1], s1b,
                                             start=True, stop=True))
            P.raw("tensor", lambda e, s2hi=s2hi: e.matmul(
                pgen[0:1, QC:2 * QC], ones_sb[:, 0:1], s2hi, start=True, stop=False))
            lm3 = t_op(lambda e, s2lo=s2lo: e.matmul(
                pgen[0:1, QC:2 * QC], ones_sb[:, 0:1], s2lo, start=False, stop=True))
            hl3 = ln_rows(lm3)
            P.sf("pg6", V, hl3)
            P.sf("pg7", V, hl3)
            for _ in range(22):
                yield "c"
            rep3, rsl3, nsl3, k13, k23 = ln_reps(hl3, 'pgen')
            sv3 = xsv
            last_n = 0
            P.wait("vector", T, rep3)
            for r in range(NCH):
                v_op(lambda e, r=r, sv3=sv3: e.tensor_tensor(
                    t0[:], sv3[:, r, :], rsl3, ALU.mult))
                last_n = v_op(lambda e, r=r: e.tensor_tensor(
                    xn_c(r), t0[:], nsl3, ALU.add))
            P.sf(k13, V, last_n)
            P.sf(k23, V, last_n)
            boundary_restore(bnd2_sb)
            yield "c"
            fres2 = {}
            yield from ffn_gen(esrc, bnd2_sb, b2_[ebuf], f"b2{ebuf}", PG_BANKS, fres2)
            y_t = fres2['out_t']
            P.wait("scalar", V, y_t)
            yst = P.dma("scalar", d_sp[ebuf], lambda e, eqc=eqc, esrc=esrc:
                e.dma_start(out=hview(yt, eqc),
                            in_=esrc[:].rearrange("p (r n) -> p r n", r=NCH)))
            if eqc + 2 < nqc:
                P.wait("sync", d_sp[ebuf], yst)
                x_load_tick[ebuf] = P.dma("sync", d_x[ebuf],
                    lambda e, eqc=eqc, ebuf=ebuf: e.dma_start(
                        out=xv(ebuf), in_=hview(x1s, eqc + 2)))

        ep_state = {"gen": None, "parked": False, "mid": False}

        def _ep_next():
            f = next(ep_state["gen"], None)
            if f is None:
                ep_state["gen"] = None
                ep_state["mid"] = False
            elif f == "b":
                ep_state["parked"] = True
                ep_state["mid"] = False
            else:
                ep_state["mid"] = (f == "f")

        def ep_step(k, unpark=False):
            if ep_state["parked"]:
                if not unpark:
                    return
                ep_state["parked"] = False
            for _ in range(k):
                if ep_state["gen"] is None or ep_state["parked"]:
                    return
                _ep_next()

        def ep_drain_coarse(cap=16):
            while (ep_state["mid"] and cap > 0 and ep_state["gen"] is not None
                   and not ep_state["parked"]):
                _ep_next()
                cap -= 1

        for qc in range(nqc):
            buf = qc % 2
            src = xb[buf]
            b2q = b2_[buf]
            b2vq = b2q[:].rearrange("p (r n) -> p r n", r=NCH)
            prev_pair = None
            last_q_mm = 0
            norm_t = 0

            def qproj(p_):
                P.wf("tensor", "pg7")
                g_ = qc * NPAIR + p_
                P.wait("tensor", d_n, slot_tick[g_ % NSLOT])
                pairch = xn1b_slot(g_)
                P.raw("tensor", lambda e, pairch=pairch: e.matmul(
                    pgen[0:64, QC:2 * QC], wqa_sb[0:64, :], pairch[0:64, :],
                    start=True, stop=True, tile_position=(0, 0)))
                qmt_ = t_op(lambda e, pairch=pairch: e.matmul(
                    pgen[64:128, QC:2 * QC], wqa_sb[64:128, :], pairch[64:128, :],
                    start=True, stop=True, tile_position=(64, 64)))
                qb_ = qbufs[p_ % 2]
                P.wait("vector", T, qmt_)
                P.wf("vector", f"qb{p_ % 2}")
                qe_ = v_op(lambda e, qb_=qb_: e.tensor_single_scalar(
                    qb_[:, 0:QC], pgen[:, QC:2 * QC], INV32, ALU.mult))
                P.sf("pg7", V, qe_)
                slot_qmt[g_ % NSLOT] = qmt_
                if g_ + NSLOT - 1 < nqc * NPAIR:
                    ring_load(g_ + NSLOT - 1)
                return qmt_, qe_

            q_next = qproj(0)
            for p in range(NPAIR):
                gidx = qc * NPAIR + p
                h0, h1 = 2 * gidx, 2 * gidx + 1
                qmt, qe = q_next
                last_q_mm = qmt
                q_sbv = qbufs[p % 2]
                ep_drain_coarse()
                if p + 1 < NPAIR:
                    q_next = qproj(p + 1)
                ep_step(1, unpark=True)
                exp_ticks = [0] * KCN
                exp_av = [0] * KCN

                def emit_av(kc):
                    sb2 = kc % 2
                    h = h0 if kc < 16 else h1
                    if kc == 0:
                        P.wf("tensor", "pavB")
                    if kc % 16 == 0:
                        P.wait("tensor", d_v[0], vht[h])
                    P.wait("tensor", SS, exp_ticks[kc])
                    vb = vph[h % 3]
                    c0 = (kc % 16) * 130
                    P.raw("tensor", lambda e, sb2=sb2, vb=vb, c0=c0: e.matmul(
                        pav[0:65, 0:QC], vb[:, c0: c0 + 65],
                        pt[sb2][:, 0:QC], start=(kc == 0), stop=(kc == KCN - 1)))
                    t_ = t_op(lambda e, sb2=sb2, vb=vb, c0=c0: e.matmul(
                        pav[0:65, QC:2 * QC], vb[:, c0 + 65: c0 + 130],
                        pt[sb2][:, QC:2 * QC], start=(kc == 0), stop=(kc == KCN - 1)))
                    exp_av[kc] = t_
                    vfree[h] = t_
                    return t_

                def emit_sc(kc):
                    sb_ = kc % 2
                    h = h0 if kc < 16 else h1
                    if kc < 2:
                        P.wf("tensor", f"pbB{sb_}")
                        P.wf("tensor", f"pb{sb_ * 2}")
                        P.wf("tensor", f"pb{sb_ * 2 + 1}")
                        if kc == 0:
                            P.wait("tensor", V, qe)
                    else:
                        P.wait("tensor", SS, exp_ticks[kc - 2])
                    if kc % 16 == 0:
                        P.wait("tensor", d_k[0], kht[h])
                    kb = kph[h % 3]
                    c0 = (kc % 16) * 128
                    P.raw("tensor", lambda e, sb_=sb_, kb=kb, c0=c0, q_sbv=q_sbv:
                        e.matmul(
                            pbig[:, sb_ * 1024: sb_ * 1024 + QC],
                            kb[0:64, c0:c0 + 128],
                            q_sbv[0:64, 0:QC], start=True, stop=True,
                            tile_position=(0, 0)))
                    mm = t_op(lambda e, sb_=sb_, kb=kb, c0=c0, q_sbv=q_sbv:
                        e.matmul(
                            pbig[:, sb_ * 1024 + QC: sb_ * 1024 + 2 * QC],
                            kb[64:128, c0:c0 + 128],
                            q_sbv[64:128, 0:QC], start=True, stop=True,
                            tile_position=(64, 0)))
                    kfree[h] = mm
                    return mm

                def emit_exp(kc, sct):
                    sb_ = kc % 2
                    P.wait("scalar", T, sct)
                    if kc >= 2:
                        P.wait("scalar", T, exp_av[kc - 2])
                    else:
                        P.wait("scalar", T, pt_guard[sb_])
                    exp_ticks[kc] = s_op(lambda e, sb_=sb_, p=p: e.activation(
                        pt[sb_][:], pbig[:, sb_ * 1024: sb_ * 1024 + 2 * QC],
                        AF.Exp, bias=negc_sb[:, p:p + 1], scale=1.0))
                    if debug and qc == 0 and p == 0:
                        P.wait("scalar", SS, exp_ticks[kc])
                        P.dma("scalar", d_w, lambda e, sb_=sb_, kc=kc: e.dma_start(
                            out=dbgp[:, kc * 1024:(kc + 1) * 1024], in_=pt[sb_][:]))
                        P.wait("scalar", d_w, d_w.n)

                for kc in range(KCN):
                    s0 = emit_sc(kc)
                    if kc >= 1:
                        emit_av(kc - 1)
                    emit_exp(kc, s0)
                    ep_step(1 + kc % 2)
                    if kc == 12 and prev_pair is not None:
                        ep_drain_coarse()
                        norm_t = norm_apply(*prev_pair, b2vq, b2q)
                        prev_pair = None
                    if kc == 17 and h0 + 3 < NH:
                        P.wait("sync", T, T.n if STRICT_KV else
                               max(kfree[h0], vfree[h0]))
                        kv_load(h0 + 3)
                av_last = emit_av(KCN - 1)
                P.sf(f"pbB0", SS, exp_ticks[KCN - 2])
                P.sf(f"pbB1", SS, exp_ticks[KCN - 1])
                pt_guard[0] = exp_av[KCN - 2]
                pt_guard[1] = exp_av[KCN - 1]
                P.sf(f"qb{p % 2}", T, av_last)
                if h1 + 3 < NH:
                    P.wait("sync", T, T.n if STRICT_KV else
                           max(kfree[h1], vfree[h1]))
                    kv_load(h1 + 3)
                ab = p % 2
                P.wait("vector", T, av_last)
                P.wf("vector", f"avout{ab}")
                ave = v_op(lambda e, ab=ab: e.tensor_copy(
                    avout[ab][0:65, 0:2 * QC], pav[0:65, 0:2 * QC]))
                P.sf("pavB", V, ave)
                if debug and qc == 0 and p == 0:
                    P.wait("scalar", V, ave)
                    P.dma("scalar", d_w, lambda e, ab=ab: e.dma_start(
                        out=dbga2, in_=avout[ab][:]))
                    P.wait("vector", d_w, d_w.n)
                if prev_pair is not None:
                    ep_drain_coarse()
                    norm_t = norm_apply(*prev_pair, b2vq, b2q)
                prev_pair = (p, ab, norm_rows(ab))
            ep_drain_coarse()
            norm_t = norm_apply(*prev_pair, b2vq, b2q)
            while ep_state["gen"] is not None:
                ep_step(100000, unpark=True)
            ep_state["gen"] = epilogue(qc, norm_t, d_o.n)
            if NO_OVERLAP:
                while ep_state["gen"] is not None:
                    ep_step(100000, unpark=True)

        while ep_state["gen"] is not None:
            ep_step(100000, unpark=True)
        P.wait("sync", d_sp[0], d_sp[0].n)
        P.wait("sync", d_sp[1], d_sp[1].n)

        P.run(block)

    return nc


_NC_CACHE = {}


def _get_nc():
    if "nc" not in _NC_CACHE:
        _NC_CACHE["nc"] = build_nc()
    return _NC_CACHE["nc"]


def make_in_maps(L0_attn_x, L0_attn_Wq, L0_attn_Wk, L0_attn_Wv, L0_attn_W_out,
                 L0_attn_b_out, L0_ffn_W_r, L0_ffn_W_k, L0_ffn_W_v, L0_ffn_mu_r,
                 L0_ffn_mu_k):
    x = np.asarray(L0_attn_x, dtype=np.float32)
    f32c = lambda a: np.ascontiguousarray(np.asarray(a, dtype=np.float32))

    def dualT(w):
        wt = f32c(np.asarray(w, dtype=np.float32).T)
        return np.ascontiguousarray(np.concatenate([wt, wt], axis=0))

    def chv(v):
        return np.ascontiguousarray(np.asarray(v, dtype=np.float32).reshape(8, 128).T)

    base = {
        "wrt": f32c(np.asarray(L0_ffn_W_r).T),
        "wkt": f32c(np.asarray(L0_ffn_W_k).T),
        "wvt": f32c(np.asarray(L0_ffn_W_v).T),
        "woutt": f32c(np.asarray(L0_attn_W_out).T),
        "wqa": dualT(L0_attn_Wq),
        "wka": dualT(L0_attn_Wk),
        "wva": dualT(L0_attn_Wv),
        "boutp": chv(L0_attn_b_out),
        "murh": chv(L0_ffn_mu_r),
        "mukh": chv(L0_ffn_mu_k),
    }
    in_maps = []
    for b in range(8):
        m = dict(base)
        m["xt"] = np.ascontiguousarray(x[b].T)
        in_maps.append(m)
    return in_maps


def kernel(**inputs):
    in_maps = make_in_maps(**inputs)
    nc = _get_nc()
    res = run_bass_kernel_spmd(nc, in_maps, core_ids=list(range(8)))
    out = np.empty((8, S, C), dtype=np.float32)
    for b in range(8):
        out[b] = res.results[b]["yt"].T
    return out
